# version 17
# baseline (speedup 1.0000x reference)
import numpy as np
from contextlib import ExitStack

import concourse.bass as bass
import concourse.mybir as mybir
from concourse.bass_utils import run_bass_kernel_spmd

F32 = mybir.dt.float32
BF16 = mybir.dt.bfloat16
AF = mybir.ActivationFunctionType
ALU = mybir.AluOpType
AX = mybir.AxisListType

D = 1024
NF = 33
DH = 64
FF = 4096


class Sched:

    STREAMS = ("t", "v", "s", "g", "y")

    def __init__(self):
        self.ops = {k: [] for k in self.STREAMS}
        self.cnt = {}
        self._lastw = {}

    def _emit(self, stream, fn):
        self.ops[stream].append(("op", None, fn))

    def t(self, fn):
        self._emit("t", fn)

    def v(self, fn):
        self._emit("v", fn)

    def s(self, fn):
        self._emit("s", fn)

    def g(self, fn):
        self._emit("g", fn)

    def y(self, fn):
        self._emit("y", fn)

    def mark(self, sem):
        if sem.startswith("gd") or sem == "gc":
            stream = "g"
        elif sem.startswith("y"):
            stream = "y"
        else:
            stream = sem
        inc = 16 if (sem.startswith("gd") or sem.startswith("y")) else 1
        for kind_, sem_, _v in reversed(self.ops[stream]):
            if kind_ == "op":
                break
            if kind_ == "inc":
                if sem_ == sem:
                    return (sem, self.cnt[sem])
                raise RuntimeError(f"two sems on one op: {sem_} vs {sem}")
        self.cnt[sem] = self.cnt.get(sem, 0) + inc
        self.ops[stream].append(("inc", sem, inc))
        return (sem, self.cnt[sem])

    def wait(self, stream, tok):
        if tok is None:
            return
        sem, val = tok
        key = (stream, sem)
        if self._lastw.get(key, -1) >= val:
            return
        self._lastw[key] = val
        self.ops[stream].append(("wait", sem, val))


def build(cfg):
    S = cfg["S"]
    T = cfg["T"]
    NBLK = S // 512
    NOWN = T // 512
    KC = S // 128
    NQP = T // 1024
    assert NOWN * 512 == T and NQP * 1024 == T and NBLK * 512 == S

    nc = bass.Bass("TRN2", target_bir_lowering=False)

    xT = nc.dram_tensor("xT", [D, S], F32, kind="ExternalInput").ap()
    x_own = nc.dram_tensor("x_own", [T, D], F32, kind="ExternalInput").ap()
    wqkv = nc.dram_tensor("wqkv", [D, 3 * D], F32, kind="ExternalInput").ap()
    wout = nc.dram_tensor("wout", [D, D], F32, kind="ExternalInput").ap()
    bout = nc.dram_tensor("bout", [1, D], F32, kind="ExternalInput").ap()
    wfcT = nc.dram_tensor("wfcT", [D, FF], F32, kind="ExternalInput").ap()
    wprojT = nc.dram_tensor("wprojT", [FF, D], F32, kind="ExternalInput").ap()
    mask_l = nc.dram_tensor("mask_l", [NF, 1], F32, kind="ExternalInput").ap()
    ccos = nc.dram_tensor("ccos", [128, 128], F32, kind="ExternalInput").ap()
    csin = nc.dram_tensor("csin", [128, 128], F32, kind="ExternalInput").ap()
    wvec = nc.dram_tensor("wvec", [128, 1], F32, kind="ExternalInput").ap()
    out_d = nc.dram_tensor("out", [T, D], F32, kind="ExternalOutput").ap()
    kscr = nc.dram_tensor("kscr", [D, S], BF16, kind="Internal").ap()
    vscr = nc.dram_tensor("vscr", [S, D], BF16, kind="Internal").ap()
    wfcb = nc.dram_tensor("wfcb", [D, FF], BF16, kind="Internal").ap()
    qscr = nc.dram_tensor("qscr", [D, T], BF16, kind="Internal").ap()
    sscr = nc.dram_tensor("sscr", [T, D], BF16, kind="Internal").ap()
    wpjb = nc.dram_tensor("wpjb", [FF, D], BF16, kind="Internal").ap()

    S_ = Sched()
    VW = KC * 2 * 68

    with ExitStack() as ctx:
        e = ctx.enter_context
        xn_own = e(nc.sbuf_tensor("xn_own", [128, 8 * T], BF16))
        o_sb = e(nc.sbuf_tensor("o_sb", [128, 8 * T], BF16))
        bb_sb = e(nc.sbuf_tensor("bb_sb", [128, D], F32))
        bias_sb = e(nc.sbuf_tensor("bias_sb", [128, 1], F32))
        g_sb = e(nc.sbuf_tensor("g_sb", [128, 128], BF16))
        ones_bf = e(nc.sbuf_tensor("ones_bf", [128, 1], BF16))
        ones_f = e(nc.sbuf_tensor("ones_f", [1, 128], F32))
        ps = e(nc.psum_tensor("ps", [128, 4096], F32))

        def bank(b, n=1):
            return ps[:, b * 512:(b + n) * 512]

        from concourse.hw_specs import get_activation_tables

        def preload_set(name):
            idx = list(get_activation_tables(nc.m.arch).keys()).index(name)
            return nc.scalar.add_instruction(mybir.InstLoadActFuncSet(
                name=nc.get_next_instruction_name(), act_func_set_id=idx,
                ins=[], outs=[]))

        sems = {k: e(nc.semaphore("sem_" + k)) for k in (
            "t", "v", "s", "gc", "y_const", "gd_mask", "gd_wqkv", "gd_wout",
            "gd_wfc0", "gd_wfc1", "gd_x0", "gd_x1", "gd_x2", "y_kst0", "y_kst1",
            "y_vst0", "y_vst1", "y_kld0", "y_kld1", "gd_vld0", "gd_vld1",
            "gd_wpj0", "gd_wpj1", "y_xtm", "y_out", "gd_wcast",
            "y_wfc0", "y_wfc1", "y_qst0", "y_qst1", "y_k0", "y_k1",
            "gd_v0", "gd_v1", "y_q0", "y_q1", "y_ss", "y_ss0", "y_ss1",
            "y_out0", "y_out1", "y_k2", "gd_v2")}
        MM = nc.tensor.matmul

        with ExitStack() as qp_ctx:
            qst_sb = qp_ctx.enter_context(nc.sbuf_tensor("qst_sb", [128, 2 * 512], BF16))

            with ExitStack() as c1:
                e1 = c1.enter_context
                wqkv_sb = e1(nc.sbuf_tensor("wqkv_sb", [128, 8 * 3 * D], BF16))
                brow = e1(nc.sbuf_tensor("brow", [1, D], F32))
                x_sb = e1(nc.sbuf_tensor("x_sb", [128, 3 * 4096], BF16))
                xsq_sb = e1(nc.sbuf_tensor("xsq_sb", [128, 2 * 512], BF16))
                xnt_sb = e1(nc.sbuf_tensor("xnt_sb", [128, 1024], BF16))
                kst_sb = e1(nc.sbuf_tensor("kst_sb", [128, 2 * 512], BF16))
                vst_sb = e1(nc.sbuf_tensor("vst_sb", [128, 2 * 1024], BF16))
                qt_sb = e1(nc.sbuf_tensor("qt_sb", [128, 512], BF16))
                sq_sb = e1(nc.sbuf_tensor("sq_sb", [128, 2 * 512], BF16))
                mb_sb = e1(nc.sbuf_tensor("mb_sb", [128, 1024], BF16))
                rb_sb = e1(nc.sbuf_tensor("rb_sb", [128, 1024], F32))
                mrow = e1(nc.sbuf_tensor("mrow", [1, 512], F32))
                r1 = e1(nc.sbuf_tensor("r1", [1, 512], F32))
                r2 = e1(nc.sbuf_tensor("r2", [1, 512], F32))
                r3 = e1(nc.sbuf_tensor("r3", [1, 512], F32))
                s2_sb = e1(nc.sbuf_tensor("s2_sb", [128, 1], F32))
                cs_sb = e1(nc.sbuf_tensor("cs_sb", [128, 128], F32))
                ss_sb = e1(nc.sbuf_tensor("ss_sb", [128, 128], F32))
                ccos_sb = e1(nc.sbuf_tensor("ccos_sb", [128, 128], F32))
                csin_sb = e1(nc.sbuf_tensor("csin_sb", [128, 128], F32))
                wvec_sb = e1(nc.sbuf_tensor("wvec_sb", [128, 1], F32))
                mk2_sb = e1(nc.sbuf_tensor("mk2_sb", [1, 1], F32))
                mq2_sb = e1(nc.sbuf_tensor("mq2_sb", [1, 1], F32))
                t11 = e1(nc.sbuf_tensor("t11", [1, 8], F32))
                eps_sb = e1(nc.sbuf_tensor("eps_sb", [1, 1], F32))

                S_.y(lambda: nc.sync.dma_start(ccos_sb[:], ccos))
                S_.mark("y_const")
                S_.y(lambda: nc.sync.dma_start(csin_sb[:], csin))
                S_.mark("y_const")
                S_.y(lambda: nc.sync.dma_start(wvec_sb[:], wvec))
                S_.mark("y_const")
                S_.y(lambda: nc.sync.dma_start(brow[:], bout))
                tk_consts = S_.mark("y_const")

                S_.v(lambda: nc.vector.memset(s2_sb[:], 0.0))
                tk_ms = S_.mark("v")
                S_.wait("g", tk_ms)
                S_.g(lambda: nc.gpsimd.dma_start(s2_sb[0:NF, :], mask_l))
                S_.mark("gd_mask")
                S_.g(lambda: nc.gpsimd.dma_start(s2_sb[64:64 + NF, :], mask_l))
                tk_mask = S_.mark("gd_mask")

                S_.v(lambda: nc.vector.memset(ones_bf[:], 1.0))
                S_.v(lambda: nc.vector.memset(ones_f[:], 1.0))
                S_.v(lambda: nc.vector.memset(mk2_sb[:], 0.0))
                S_.v(lambda: nc.vector.memset(mq2_sb[:], 0.0))
                S_.v(lambda: nc.vector.memset(eps_sb[:], 1e-5))
                tk_ones = S_.mark("v")

                S_.s(lambda: nc.scalar.preload_act_table(AF.Sigmoid))
                S_.wait("s", tk_mask)
                S_.s(lambda: nc.scalar.activation(s2_sb[:], s2_sb[:], AF.Sigmoid))
                tk_sg = S_.mark("s")
                S_.wait("v", tk_sg)
                S_.v(lambda: nc.vector.tensor_mul(s2_sb[:], s2_sb[:], s2_sb[:]))
                t_ = S_.mark("v")
                S_.wait("v", t_)
                S_.wait("v", tk_consts)
                S_.v(lambda: nc.vector.tensor_scalar_mul(s2_sb[:], s2_sb[:], wvec_sb[:]))
                t_ = S_.mark("v")
                S_.wait("v", t_)
                S_.v(lambda: nc.vector.tensor_scalar_mul(cs_sb[:], ccos_sb[:], s2_sb[:]))
                S_.v(lambda: nc.vector.tensor_scalar_mul(ss_sb[:], csin_sb[:], s2_sb[:]))
                tk_cs = S_.mark("v")
                S_.wait("t", tk_cs)
                S_.t(lambda: MM(ps[0:128, 0:128], cs_sb[:], ccos_sb[:],
                                start=True, stop=False))
                S_.t(lambda: MM(ps[0:128, 0:128], ss_sb[:], csin_sb[:],
                                start=False, stop=True))
                tk_gmm = S_.mark("t")
                S_.wait("v", tk_gmm)
                S_.v(lambda: nc.vector.tensor_copy(g_sb[:], ps[0:128, 0:128]))
                tk_g = S_.mark("v")
                S_.wait("t", tk_g)
                S_.t(lambda: MM(bank(0), ones_f[:], brow[0:1, 0:512],
                                start=True, stop=True))
                S_.t(lambda: MM(bank(1), ones_f[:], brow[0:1, 512:1024],
                                start=True, stop=True))
                tk_bbm = S_.mark("t")
                S_.wait("v", tk_bbm)
                S_.v(lambda: nc.vector.tensor_copy(bb_sb[:, 0:512], bank(0)))
                S_.v(lambda: nc.vector.tensor_copy(bb_sb[:, 512:1024], bank(1)))
                tk_bb = S_.mark("v")

                S_.s(lambda: preload_set("natural_log_exp_and_others"))

                tok_xn = [None] * NBLK
                tk_kst_dma = [None, None]
                tk_qst_dma = [None, None]
                tk_vst_dma = [None, None]
                prev_m1 = None
                prev_lnmm = None
                tk_kn2v = None
                tk_qn2v = None
                prev_kev = None
                prev_vev = None
                prev_ksq_mm = None
                prev_qg = None
                prev_qpev = None
                tk_blk_t = [None] * NBLK

                xv_save = {}

                def ln_stats(blk):
                    nonlocal prev_m1
                    par = blk % 3
                    xv = x_sb[:, par * 4096:par * 4096 + 4096]
                    xv_save[blk] = xv
                    srcx = xT.rearrange("(dc p) s -> p dc s", p=128)[
                        :, :, blk * 512:(blk + 1) * 512]
                    dstx = xv.rearrange("p (dc s) -> p dc s", dc=8)
                    if blk >= 3:
                        S_.wait("g", tok_xn[blk - 3])
                        S_.wait("g", tk_blk_t[blk - 3])
                    S_.g(lambda s=srcx, d=dstx: nc.gpsimd.dma_start(d, s))
                    tk_x = S_.mark(("gd_x0", "gd_x1", "gd_x2")[par])

                    S_.wait("t", tk_x)
                    S_.wait("t", tk_ones)
                    if blk == 0:
                        S_.wait("t", tk_bb)
                    if prev_m1 is not None:
                        S_.wait("t", prev_m1)
                    for dc in range(8):
                        S_.t(lambda r=xv[:, dc * 512:(dc + 1) * 512],
                             st=(dc == 0), sp=(dc == 7):
                             MM(bank(0)[0:1, :], ones_bf[:], r, start=st, stop=sp))
                    tk_sx = S_.mark("t")
                    tk_xm = {}
                    tk_x2mm = {}
                    for dc in range(8):
                        xq = xsq_sb[:, (dc % 2) * 512:(dc % 2) * 512 + 512]
                        if dc >= 2:
                            S_.wait("v", tk_x2mm[dc - 2])
                        S_.wait("v", tk_x)
                        S_.v(lambda a=xq, b=xv[:, dc * 512:(dc + 1) * 512]:
                             nc.vector.tensor_mul(a, b, b))
                        tk_xm[dc] = S_.mark("v")
                        if dc >= 1:
                            S_.wait("t", tk_xm[dc - 1])
                            S_.t(lambda r=xsq_sb[:, ((dc - 1) % 2) * 512:
                                              ((dc - 1) % 2) * 512 + 512],
                                 st=(dc - 1 == 0), sp=False:
                                 MM(bank(1)[0:1, :], ones_bf[:], r, start=st, stop=sp))
                            tk_x2mm[dc - 1] = S_.mark("t")
                    S_.wait("t", tk_xm[7])
                    S_.t(lambda r=xsq_sb[:, (7 % 2) * 512:(7 % 2) * 512 + 512]:
                         MM(bank(1)[0:1, :], ones_bf[:], r, start=False, stop=True))
                    tk_x2mm[7] = S_.mark("t")
                    if prev_lnmm is not None:
                        S_.wait("v", prev_lnmm)
                    S_.wait("v", tk_sx)
                    S_.wait("v", tk_x2mm[7])
                    S_.v(lambda: nc.vector.tensor_scalar_mul(mrow[:], bank(0)[0:1, :], 1.0 / D))
                    S_.v(lambda: nc.vector.tensor_scalar_mul(r1[:], bank(1)[0:1, :], 1.0 / D))
                    prev_m1 = S_.mark("v")
                    S_.wait("v", prev_m1)
                    S_.v(lambda: nc.vector.tensor_mul(r2[:], mrow[:], mrow[:]))
                    t_ = S_.mark("v")
                    S_.wait("v", t_)
                    S_.v(lambda: nc.vector.tensor_sub(r1[:], r1[:], r2[:]))
                    tk_var = S_.mark("v")
                    S_.wait("s", tk_var)
                    S_.wait("s", tk_ones)
                    S_.s(lambda: nc.scalar.activation(r2[:], r1[:], AF.Ln, bias=eps_sb[:]))
                    t_ = S_.mark("s")
                    S_.wait("s", t_)
                    S_.s(lambda: nc.scalar.activation(r3[:], r2[:], AF.Exp, scale=-0.5))
                    tk_rstd_a[blk] = S_.mark("s")

                def ln_apply(blk):
                    nonlocal prev_lnmm
                    xv = xv_save[blk]
                    S_.wait("t", tk_rstd_a[blk])
                    if tk_kn2v is not None:
                        S_.wait("t", tk_kn2v)
                    if tk_qn2v is not None:
                        S_.wait("t", tk_qn2v)
                    S_.t(lambda: MM(bank(2), ones_f[:], mrow[:], start=True, stop=True))
                    S_.t(lambda: MM(bank(3), ones_f[:], r3[:], start=True, stop=True))
                    prev_lnmm = S_.mark("t")
                    S_.wait("s", prev_lnmm)
                    if blk > 0:
                        S_.wait("s", tok_xn[blk - 1])
                    S_.s(lambda: nc.scalar.activation(mb_sb[:, 0:512], bank(2), AF.Copy))
                    S_.s(lambda: nc.scalar.activation(mb_sb[:, 512:1024], bank(2), AF.Copy))
                    S_.s(lambda: nc.scalar.activation(rb_sb[:, 0:512], bank(3), AF.Copy))
                    S_.s(lambda: nc.scalar.activation(rb_sb[:, 512:1024], bank(3), AF.Copy))
                    tk_mbc = S_.mark("s")
                    if blk < NOWN:
                        xn_dst = xn_own[:, blk * 4096:(blk + 1) * 4096]
                    else:
                        xn_dst = xv
                    S_.wait("v", tk_mbc)
                    for dc2 in range(4):
                        xc = xv[:, dc2 * 1024:(dc2 + 1) * 1024]
                        xo = xn_dst[:, dc2 * 1024:(dc2 + 1) * 1024]
                        S_.v(lambda a=xnt_sb[:], b=xc:
                             nc.vector.tensor_sub(a, b, mb_sb[:]))
                        t_ = S_.mark("v")
                        S_.wait("v", t_)
                        S_.v(lambda a=xo, b=xnt_sb[:]:
                             nc.vector.tensor_mul(a, b, rb_sb[:]))
                        t_ = S_.mark("v")
                        S_.wait("v", t_)
                    tok_xn[blk] = t_
                    xn_save[blk] = xn_dst

                def stage_kvq(blk):
                    nonlocal tk_kn2v, tk_qn2v, prev_kev, prev_vev, prev_ksq_mm, prev_qg, prev_qpev
                    par = blk % 2
                    xn_src = xn_save[blk]
                    tk_ksq = {}
                    S_.wait("t", tok_xn[blk])
                    S_.wait("t", tk_wqkv)
                    if prev_qpev is not None:
                        S_.wait("t", prev_qpev)
                    kev_par = getattr(S_, "_kev_par", [None, None, None, None])
                    S_._kev_par = kev_par
                    for dc in range(8):
                        kbk = 4 + dc % 4
                        kdst = kst_sb[:, (dc % 2) * 512:(dc % 2) * 512 + 512]
                        if kev_par[dc % 4] is not None:
                            S_.wait("t", kev_par[dc % 4])
                        for di in range(8):
                            lhs = wqkv_sb[:, di * 3 * D + D + dc * 128:
                                          di * 3 * D + D + (dc + 1) * 128]
                            rhs = xn_src[:, di * 512:(di + 1) * 512]
                            S_.t(lambda l=lhs, r=rhs, b=kbk, st=(di == 0), sp=(di == 7):
                                 MM(bank(b), l, r, start=st, stop=sp))
                        tk_kmm = S_.mark("t")
                        S_.wait("s", tk_kmm)
                        S_.wait("s", tk_kst_dma[dc % 2])
                        S_.s(lambda d=kdst, b=kbk: nc.scalar.activation(d, bank(b), AF.Copy))
                        prev_kev = S_.mark("s")
                        kev_par[dc % 4] = prev_kev
                        if prev_ksq_mm is not None:
                            S_.wait("v", prev_ksq_mm)
                        S_.wait("v", prev_kev)
                        S_.v(lambda d=kdst, q=sq_sb[:, (dc % 2) * 512:(dc % 2) * 512 + 512]:
                             nc.vector.tensor_mul(q, d, d))
                        tk_ksq[dc] = S_.mark("v")
                        if dc >= 1:
                            S_.wait("t", tk_ksq[dc - 1])
                            S_.t(lambda q=sq_sb[:, ((dc - 1) % 2) * 512:
                                              ((dc - 1) % 2) * 512 + 512],
                                 st=(dc - 1 == 0):
                                 MM(bank(2)[0:1, :], ones_bf[:], q, start=st, stop=False))
                            prev_ksq_mm = S_.mark("t")
                        S_.wait("y", prev_kev)
                        kdd = kscr.rearrange("(dc p) s -> dc p s", p=128)[dc][
                            :, blk * 512:(blk + 1) * 512]
                        S_.y(lambda dd=kdd, sd=kdst: nc.sync.dma_start(dd, sd))
                        tk_kst_dma[dc % 2] = S_.mark("y_kst0" if dc % 2 == 0 else "y_kst1")
                    S_.wait("t", tk_ksq[7])
                    S_.t(lambda q=sq_sb[:, (7 % 2) * 512:(7 % 2) * 512 + 512]:
                         MM(bank(2)[0:1, :], ones_bf[:], q, start=False, stop=True))
                    prev_ksq_mm = S_.mark("t")
                    if tk_kn2v is not None:
                        S_.wait("v", tk_kn2v)
                    S_.wait("v", prev_ksq_mm)
                    S_.v(lambda: nc.vector.reduce_max(t11[0:1, 0:1], bank(2)[0:1, :], axis=AX.X))
                    t_ = S_.mark("v")
                    S_.wait("v", t_)
                    S_.v(lambda: nc.vector.tensor_max(mk2_sb[:], mk2_sb[:], t11[0:1, 0:1]))
                    tk_kn2v = S_.mark("v")

                    vev_par = kev_par
                    for tq in range(4):
                        vdst = vst_sb[:, (tq % 2) * 1024:(tq % 2) * 1024 + 1024]
                        for nn in range(2):
                            gidx = (tq * 2 + nn) % 4
                            vbk = 4 + gidx
                            if vev_par[gidx] is not None:
                                S_.wait("t", vev_par[gidx])
                            for di in range(8):
                                lhs = xn_src[:, di * 512 + tq * 128: di * 512 + (tq + 1) * 128]
                                rh = wqkv_sb[:, di * 3 * D + 2 * D + nn * 512:
                                             di * 3 * D + 2 * D + (nn + 1) * 512]
                                S_.t(lambda l=lhs, r=rh, b=vbk, st=(di == 0), sp=(di == 7):
                                     MM(bank(b), l, r, start=st, stop=sp))
                            tk_vmm = S_.mark("t")
                            S_.wait("s", tk_vmm)
                            S_.wait("s", tk_vst_dma[tq % 2])
                            S_.s(lambda d=vdst, n=nn, b=vbk:
                                 nc.scalar.activation(d[:, n * 512:(n + 1) * 512],
                                                      bank(b), AF.Copy))
                            prev_vev = S_.mark("s")
                            vev_par[gidx] = prev_vev
                        S_.wait("y", prev_vev)
                        vdd = vscr[blk * 512 + tq * 128: blk * 512 + (tq + 1) * 128, :]
                        S_.y(lambda dd=vdd, sd=vdst: nc.sync.dma_start(dd, sd))
                        tk_vst_dma[tq % 2] = S_.mark("y_vst0" if tq % 2 == 0 else "y_vst1")

                    if blk < NOWN:
                        tk_qn2_mm = None
                        tk_qsq = {}
                        for hp in range(8):
                            if prev_qg is not None:
                                S_.wait("t", prev_qg)
                            S_.wait("t", kev_par[3])
                            for di in range(8):
                                lhs = wqkv_sb[:, di * 3 * D + hp * 128:
                                              di * 3 * D + (hp + 1) * 128]
                                rhs = xn_src[:, di * 512:(di + 1) * 512]
                                S_.t(lambda l=lhs, r=rhs, st=(di == 0), sp=(di == 7):
                                     MM(bank(7), l, r, start=st, stop=sp))
                            tk_qmm = S_.mark("t")
                            S_.wait("s", tk_qmm)
                            if prev_qg is not None:
                                S_.wait("s", prev_qg)
                            S_.s(lambda: nc.scalar.activation(qt_sb[:], bank(7), AF.Copy))
                            tk_qev = S_.mark("s")
                            S_.wait("t", tk_qev)
                            S_.wait("t", tk_g)
                            S_.wait("t", kev_par[0])
                            if prev_qpev is not None:
                                S_.wait("t", prev_qpev)
                            S_.t(lambda: MM(bank(4), g_sb[:], qt_sb[:],
                                            start=True, stop=True))
                            prev_qg = S_.mark("t")
                            S_.wait("s", prev_qg)
                            qdst = qst_sb[:, (hp % 2) * 512:(hp % 2) * 512 + 512]
                            S_.wait("s", tk_qst_dma[hp % 2])
                            S_.s(lambda d=qdst: nc.scalar.activation(d, bank(4), AF.Copy))
                            prev_qpev = S_.mark("s")
                            S_.wait("y", prev_qpev)
                            qdd = qscr[hp * 128:(hp + 1) * 128,
                                       blk * 512:(blk + 1) * 512]
                            S_.y(lambda dd=qdd, sd=qdst: nc.sync.dma_start(dd, sd))
                            tk_qst_dma[hp % 2] = S_.mark(
                                "y_qst0" if hp % 2 == 0 else "y_qst1")
                            if prev_ksq_mm is not None:
                                S_.wait("v", prev_ksq_mm)
                            if tk_qn2_mm is not None:
                                S_.wait("v", tk_qn2_mm)
                            S_.wait("v", prev_qpev)
                            S_.v(lambda d=qdst, q=sq_sb[:, (hp % 2) * 512:
                                              (hp % 2) * 512 + 512]:
                                 nc.vector.tensor_mul(q, d, d))
                            tk_qsq[hp] = S_.mark("v")
                            if hp >= 1:
                                S_.wait("t", tk_qsq[hp - 1])
                                S_.t(lambda q=sq_sb[:, ((hp - 1) % 2) * 512:
                                                  ((hp - 1) % 2) * 512 + 512],
                                     st=(hp - 1 == 0):
                                     MM(bank(3)[0:1, :], ones_bf[:], q,
                                        start=st, stop=False))
                                tk_qn2_mm = S_.mark("t")
                                prev_ksq_mm = tk_qn2_mm
                        S_.wait("t", tk_qsq[7])
                        S_.t(lambda q=sq_sb[:, (7 % 2) * 512:(7 % 2) * 512 + 512]:
                             MM(bank(3)[0:1, :], ones_bf[:], q, start=False, stop=True))
                        tk_qn2_mm = S_.mark("t")
                        prev_ksq_mm = tk_qn2_mm
                        S_.wait("v", tk_qn2_mm)
                        S_.v(lambda: nc.vector.reduce_max(t11[0:1, 1:2], bank(3)[0:1, :],
                                                          axis=AX.X))
                        t_ = S_.mark("v")
                        S_.wait("v", t_)
                        S_.v(lambda: nc.vector.tensor_max(mq2_sb[:], mq2_sb[:], t11[0:1, 1:2]))
                        tk_qn2v = S_.mark("v")
                    tk_blk_t[blk] = S_.mark("t")

                xn_save = {}
                tk_rstd_a = {}
                ln_stats(0)
                for dc in range(8):
                    srcw = wqkv.rearrange("(dc p) n -> dc p n", p=128)[dc]
                    dstw = wqkv_sb[:, dc * 3 * D:(dc + 1) * 3 * D]
                    S_.g(lambda s=srcw, d=dstw: nc.gpsimd.dma_start(d, s))
                    tk_wqkv = S_.mark("gd_wqkv")
                ln_apply(0)
                if NBLK > 1:
                    ln_stats(1)
                for ch in range(8):
                    S_.g(lambda c=ch: nc.gpsimd.dma_start(
                        wfcb[c * 128:(c + 1) * 128, :], wfcT[c * 128:(c + 1) * 128, :]))
                    tk_wcast = S_.mark("gd_wcast")
                for ch in range(8):
                    S_.g(lambda c=ch: nc.gpsimd.dma_start(
                        wpjb[c * 512:(c + 1) * 512, :], wprojT[c * 512:(c + 1) * 512, :]))
                    tk_wcast = S_.mark("gd_wcast")
                for blk in range(NBLK):
                    stage_kvq(blk)
                    if blk + 1 < NBLK:
                        ln_apply(blk + 1)
                    if blk + 2 < NBLK:
                        ln_stats(blk + 2)

                tk_qst_fin = [("y_qst0", S_.cnt.get("y_qst0", 0)),
                              ("y_qst1", S_.cnt.get("y_qst1", 0))]
                tk_kst_fin = [("y_kst0", S_.cnt.get("y_kst0", 0)),
                              ("y_kst1", S_.cnt.get("y_kst1", 0))]
                tk_vst_fin = [("y_vst0", S_.cnt.get("y_vst0", 0)),
                              ("y_vst1", S_.cnt.get("y_vst1", 0))]
                S_.wait("v", tk_kn2v)
                S_.wait("v", tk_qn2v)
                S_.v(lambda: nc.vector.tensor_mul(t11[0:1, 2:3], mq2_sb[:], mk2_sb[:]))
                t_ = S_.mark("v")
                S_.wait("s", t_)
                S_.s(lambda: nc.scalar.activation(t11[0:1, 4:5], t11[0:1, 2:3], AF.Ln))
                t_ = S_.mark("s")
                S_.wait("s", t_)
                S_.s(lambda: nc.scalar.activation(t11[0:1, 3:4], t11[0:1, 4:5], AF.Exp,
                                                  scale=0.5))
                tk_M = S_.mark("s")
                S_.wait("t", tk_M)
                S_.wait("t", tk_kn2v)
                S_.t(lambda: MM(ps[0:128, 0:1], ones_f[:], t11[0:1, 3:4],
                                start=True, stop=True))
                t_ = S_.mark("t")
                S_.wait("v", t_)
                S_.v(lambda: nc.vector.tensor_scalar_mul(bias_sb[:], ps[0:128, 0:1], -1.0))
                tk_bias = S_.mark("v")

            tk_p1_t = ("t", S_.cnt["t"])
            with ExitStack() as c3:
                e3 = c3.enter_context
                kch_sb = e3(nc.sbuf_tensor("kch_sb", [128, 3 * 2048], BF16))
                v65_sb = e3(nc.sbuf_tensor("v65_sb", [128, 3 * 16 * 136], BF16))
                qv_sb = e3(nc.sbuf_tensor("qv_sb", [128, 2 * 512], BF16))
                p_sb = e3(nc.sbuf_tensor("p_sb", [128, 2 * 1024], BF16))
                rcl_sb = e3(nc.sbuf_tensor("rcl_sb", [1, 512], F32))
                rcl2_sb = e3(nc.sbuf_tensor("rcl2_sb", [1, 512], F32))
                rbc_sb = e3(nc.sbuf_tensor("rbc_sb", [128, 512], F32))
                wfc_sb = e3(nc.sbuf_tensor("wfc_sb", [128, 2 * 8 * 512], BF16))
                wpj_sb = e3(nc.sbuf_tensor("wpj_sb", [128, 2 * 2 * 1024], BF16))
                wout_sb = e3(nc.sbuf_tensor("wout_sb", [128, 8 * D], BF16))
                ss_ld = e3(nc.sbuf_tensor("ss_ld", [128, 1024], BF16))
                x_tm = e3(nc.sbuf_tensor("x_tm", [128, D], F32))
                eo_sb = e3(nc.sbuf_tensor("eo_sb", [128, D], F32))
                h_sb = e3(nc.sbuf_tensor("h_sb", [128, 32 * 512], BF16))
                ssb_st = e3(nc.sbuf_tensor("ssb_st", [128, 1024], BF16))
                y_sb = e3(nc.sbuf_tensor("y_sb", [128, D], F32))
                mem_sb = e3(nc.sbuf_tensor("mem_sb", [128, D], F32))
                spk_sb = e3(nc.sbuf_tensor("spk_sb", [128, D], F32))
                ssum_sb = e3(nc.sbuf_tensor("ssum_sb", [128, D], F32))

                S_.wait("g", tk_bias)
                for dc in range(8):
                    srcw = wout.rearrange("(dc p) n -> dc p n", p=128)[dc]
                    dstw = wout_sb[:, dc * D:(dc + 1) * D]
                    S_.g(lambda s=srcw, d=dstw: nc.gpsimd.dma_start(d, s))
                    tk_wout = S_.mark("gd_wout")

                ffn = {"wfc": [None, None], "wfc_use": [None, None],
                       "wpj": [None, None], "wpj_use": [None, None],
                       "hev": [None, None], "sq_last": None,
                       "yev_last": None, "hall": None, "ybank": None,
                       "spk_end": None}

                def load_wfc(hf):
                    if hf >= NOWN * 8:
                        return
                    fb2, g = divmod(hf, 8)
                    par = hf % 2
                    S_.wait("y", ffn["wfc_use"][par])
                    S_.wait("y", tk_wcast)
                    if hf <= 1:
                        S_.wait("y", tk_bias)
                    srcf = wfcb.rearrange("(dc p) n -> p dc n", p=128)[
                        :, :, g * 512:(g + 1) * 512]
                    dstf = wfc_sb[:, par * 4096:(par + 1) * 4096].rearrange(
                        "p (dc c) -> p dc c", dc=8)
                    S_.y(lambda s=srcf, d=dstf: nc.sync.dma_start(d, s))
                    ffn["wfc"][par] = S_.mark("y_wfc0" if par == 0 else "y_wfc1")

                def load_wpj(pf):
                    if pf >= NOWN * 4 * 16:
                        return
                    g = pf % 16
                    par = pf % 2
                    S_.wait("g", ffn["wpj_use"][par])
                    S_.wait("g", tk_wcast)
                    S_.wait("g", tk_bias)
                    srcp = wpjb.rearrange("(f p) n -> p f n", p=128)[
                        :, g * 2:(g + 1) * 2, :]
                    dstp = wpj_sb[:, par * 2048:(par + 1) * 2048].rearrange(
                        "p (f c) -> p f c", f=2)
                    S_.g(lambda s=srcp, d=dstp: nc.gpsimd.dma_start(d, s))
                    ffn["wpj"][par] = S_.mark("gd_wpj0" if par == 0 else "gd_wpj1")

                def h_unit(fb, fc, half):
                    hf = fb * 8 + fc // 4
                    par = hf % 2
                    hb = 6 + (fc % 2)
                    if hf == 0 and fc == 0 and half == 0:
                        load_wfc(0)
                        load_wfc(1)
                    S_.wait("t", ffn["wfc"][par])
                    if half == 0:
                        S_.wait("t", ffn["hev"][fc % 2])
                    if fb == 0 and fc == 0:
                        S_.wait("t", tk_bias)
                    for di in range(half * 4, half * 4 + 4):
                        lhs = wfc_sb[:, par * 4096 + di * 512 + (fc % 4) * 128:
                                     par * 4096 + di * 512 + (fc % 4 + 1) * 128]
                        rhs = xn_own[:, fb * 4096 + di * 512:
                                     fb * 4096 + (di + 1) * 512]
                        S_.t(lambda l=lhs, r=rhs, b=hb, st=(di == 0), sp=(di == 7):
                             MM(bank(b), l, r, start=st, stop=sp))
                    tk_hmm = S_.mark("t")
                    if half == 0:
                        return
                    if fc % 4 == 3:
                        ffn["wfc_use"][par] = tk_hmm
                        load_wfc(hf + 2)
                    S_.wait("v", tk_hmm)
                    S_.wait("v", ffn["ybank"])
                    S_.wait("v", ffn["sq_last"])
                    hro = h_sb[:, fc * 512:(fc + 1) * 512]
                    S_.v(lambda o=hro, b=hb: nc.vector.tensor_scalar(
                        o, bank(b), 0.0, None, ALU.max))
                    t_ = S_.mark("v")
                    ffn["hev"][fc % 2] = t_
                    S_.wait("v", t_)
                    S_.v(lambda o=hro: nc.vector.tensor_mul(o, o, o))
                    ffn["sq_last"] = S_.mark("v")
                    if fc == 31:
                        ffn["hall"] = ffn["sq_last"]

                def y_unit(fb, tq, g, half):
                    pf = (fb * 4 + tq) * 16 + g
                    par = pf % 2
                    if pf == 0 and half == 0:
                        load_wpj(0)
                        load_wpj(1)
                    S_.wait("t", ffn["wpj"][par])
                    S_.wait("t", ffn["hall"])
                    if g == 0 and half == 0:
                        S_.wait("t", ffn["ybank"])
                    for f2 in (half,):
                        f = g * 2 + f2
                        lhs = h_sb[:, f * 512 + tq * 128: f * 512 + (tq + 1) * 128]
                        for nn in range(2):
                            rhs = wpj_sb[:, par * 2048 + f2 * 1024 + nn * 512:
                                         par * 2048 + f2 * 1024 + (nn + 1) * 512]
                            b = 6 + nn
                            S_.t(lambda l=lhs, r=rhs, b_=b,
                                 st=(f == 0), sp=(f == 31):
                                 MM(bank(b_), l, r, start=st, stop=sp))
                    tk_ymm = S_.mark("t")
                    if half == 0:
                        return
                    ffn["wpj_use"][par] = tk_ymm
                    load_wpj(pf + 2)
                    if g == 15:
                        gtq = fb * 4 + tq
                        S_.wait("v", tk_ymm)
                        S_.wait("v", ffn["spk_end"])
                        S_.v(lambda: nc.vector.tensor_copy(y_sb[:, 0:512], bank(6)))
                        S_.v(lambda: nc.vector.tensor_copy(y_sb[:, 512:1024], bank(7)))
                        t_ = S_.mark("v")
                        ffn["ybank"] = t_
                        S_.wait("v", t_)
                        S_.v(lambda: nc.vector.tensor_scalar(
                            ssum_sb[:], y_sb[:], 1.0, None, ALU.is_gt))
                        t_ = S_.mark("v")
                        S_.wait("v", t_)
                        S_.v(lambda: nc.vector.scalar_tensor_tensor(
                            mem_sb[:], y_sb[:], 1.9, ssum_sb[:], ALU.mult,
                            ALU.subtract))
                        t_ = S_.mark("v")
                        S_.wait("v", t_)
                        S_.v(lambda: nc.vector.tensor_scalar(
                            spk_sb[:], mem_sb[:], 1.0, None, ALU.is_gt))
                        t_ = S_.mark("v")
                        S_.wait("v", t_)
                        S_.v(lambda: nc.vector.tensor_add(ssum_sb[:], ssum_sb[:],
                                                          spk_sb[:]))
                        t_ = S_.mark("v")
                        for stp in range(3):
                            S_.wait("v", t_)
                            S_.v(lambda: nc.vector.scalar_tensor_tensor(
                                y_sb[:] if False else spk_sb[:], mem_sb[:], 0.9,
                                y_sb[:], ALU.mult, ALU.add) if False else
                                nc.vector.scalar_tensor_tensor(
                                    mem_sb[:], mem_sb[:], 0.9, y_sb[:],
                                    ALU.mult, ALU.add))
                            t_ = S_.mark("v")
                            S_.wait("v", t_)
                            S_.v(lambda: nc.vector.tensor_sub(mem_sb[:], mem_sb[:],
                                                              spk_sb[:]))
                            t_ = S_.mark("v")
                            S_.wait("v", t_)
                            S_.v(lambda: nc.vector.tensor_scalar(
                                spk_sb[:], mem_sb[:], 1.0, None, ALU.is_gt))
                            t_ = S_.mark("v")
                            S_.wait("v", t_)
                            S_.v(lambda: nc.vector.tensor_add(ssum_sb[:], ssum_sb[:],
                                                              spk_sb[:]))
                            t_ = S_.mark("v")
                        S_.wait("v", t_)
                        S_.wait("v", ffn.get("ss_dma"))
                        S_.v(lambda: nc.vector.tensor_copy(ssb_st[:], ssum_sb[:]))
                        t_ = S_.mark("v")
                        ffn["spk_end"] = t_
                        S_.wait("y", t_)
                        S_.y(lambda g_=gtq: nc.sync.dma_start(
                            sscr[g_ * 128:(g_ + 1) * 128, :], ssb_st[:]))
                        ffn["ss_dma"] = S_.mark("y_ss")

                def h_full(fb, fc):
                    h_unit(fb, fc, 0)
                    h_unit(fb, fc, 1)

                def y_full(fb, tq, g):
                    y_unit(fb, tq, g, 0)
                    y_unit(fb, tq, g, 1)

                op_state = {"edone": None, "xo": None, "ssl": None,
                            "out": [None, None], "oev": None}

                def outproj_unit(gtq):
                    S_.wait("t", ffn["hev"][0])
                    S_.wait("t", ffn["hev"][1])
                    S_.wait("t", ffn["ybank"])
                    S_.wait("t", tk_wout)
                    S_.wait("t", op_state["oev"])
                    S_.wait("t", tk_bb)
                    for hp2 in range(8):
                        lhs = o_sb[:, hp2 * T + gtq * 128: hp2 * T + (gtq + 1) * 128]
                        for nn in range(2):
                            rhs = wout_sb[:, hp2 * D + nn * 512:
                                          hp2 * D + (nn + 1) * 512]
                            S_.t(lambda l=lhs, r=rhs, b=6 + nn,
                                 st=(hp2 == 0), sp=(hp2 == 7):
                                 MM(bank(b), l, r, start=st, stop=sp))
                    tk_at = S_.mark("t")
                    S_.wait("y", op_state["edone"])
                    S_.wait("y", ffn["ss_dma"])
                    S_.y(lambda g_=gtq: nc.sync.dma_start(
                        x_tm[:], x_own[g_ * 128:(g_ + 1) * 128, :]))
                    tk_xo = S_.mark("y_xtm")
                    S_.y(lambda g_=gtq: nc.sync.dma_start(
                        ss_ld[:], sscr[g_ * 128:(g_ + 1) * 128, :]))
                    tk_ssl = S_.mark("y_ss0")
                    S_.wait("v", op_state["out"][0])
                    S_.wait("v", op_state["out"][1])
                    S_.wait("v", tk_at)
                    S_.wait("v", tk_bb)
                    S_.v(lambda: nc.vector.tensor_add(eo_sb[:], bb_sb[:], bank(6, 2)))
                    tk_eo = S_.mark("v")
                    ffn["hev"][0] = tk_eo
                    ffn["hev"][1] = tk_eo
                    ffn["ybank"] = tk_eo
                    S_.wait("v", tk_eo)
                    S_.wait("v", tk_xo)
                    S_.v(lambda: nc.vector.tensor_add(eo_sb[:], eo_sb[:], x_tm[:]))
                    t_ = S_.mark("v")
                    S_.wait("v", t_)
                    S_.wait("v", tk_ssl)
                    S_.v(lambda: nc.vector.scalar_tensor_tensor(
                        eo_sb[:], ss_ld[:], 0.2, eo_sb[:], ALU.mult, ALU.add))
                    tk_eo = S_.mark("v")
                    op_state["edone"] = tk_eo
                    S_.wait("y", tk_eo)
                    S_.y(lambda g_=gtq: nc.sync.dma_start(
                        out_d[g_ * 128:(g_ + 1) * 128, :], eo_sb[:]))
                    op_state["out"][gtq % 2] = S_.mark(
                        "y_out0" if gtq % 2 == 0 else "y_out1")

                ffn_units = []
                for fb in range(NOWN):
                    for fc in range(32):
                        ffn_units.append(lambda fb_=fb, fc_=fc: h_full(fb_, fc_))
                    for tq in range(4):
                        for g in range(16):
                            ffn_units.append(lambda fb_=fb, tq_=tq, g_=g:
                                             y_full(fb_, tq_, g_))
                ffn_i = [0]
                n_steps = 8 * (T // 512) * KC
                stride = max(n_steps // (len(ffn_units) + 1), 1)
                step_no = [0]

                def pump():
                    step_no[0] += 1
                    while ffn_i[0] < len(ffn_units) and \
                            ffn_i[0] * stride < step_no[0]:
                        ffn_units[ffn_i[0]]()
                        ffn_i[0] += 1

                NQB = T // 512
                NKCG = KC // 16
                assert NKCG * 16 == KC
                flat = [(hp, qb, kcg) for qb in range(NQB) for hp in range(8)
                        for kcg in range(NKCG)]
                tk_k = {}
                tk_v = {}
                tk_q = [None, None]
                tk_k_use = {}
                tk_v_use = {}
                tk_q_use = [None, None]
                tk_oev = None
                tk_last_sE = None
                tk_last_sE2 = None
                tk_oevac = None
                tk_rbcc = [None, None]

                def load_kv(fi):
                    if fi >= len(flat):
                        return
                    hp2, qb2, kcg2 = flat[fi]
                    par = fi % 3
                    S_.wait("y", tk_k_use.get(fi - 3))
                    if fi <= 2:
                        S_.wait("y", tk_bias)
                        S_.wait("y", tk_kst_fin[0])
                        S_.wait("y", tk_kst_fin[1])
                    ksrc = kscr[hp2 * 128:(hp2 + 1) * 128,
                                kcg2 * 2048:(kcg2 + 1) * 2048]
                    S_.y(lambda d=kch_sb[:, par * 2048:(par + 1) * 2048], s=ksrc:
                         nc.sync.dma_start(d, s))
                    tk_k[fi] = S_.mark(("y_k0", "y_k1", "y_k2")[par])
                    S_.wait("g", tk_v_use.get(fi - 3))
                    if fi <= 2:
                        S_.wait("g", tk_bias)
                        S_.wait("g", tk_vst_fin[0])
                        S_.wait("g", tk_vst_fin[1])
                        vb0 = v65_sb[:, par * 2176:(par + 1) * 2176]
                        S_.g(lambda d=vb0: nc.gpsimd.memset(d, 1.0))
                        tk_vms = S_.mark("gc")
                        S_.wait("g", tk_vms)
                    vb = v65_sb[:, par * 2176:(par + 1) * 2176]
                    for hh in range(2):
                        vsrc = vscr.rearrange("(kc p) (hh d) -> p kc hh d",
                                              p=128, d=64)[
                            :, kcg2 * 16:(kcg2 + 1) * 16, 2 * hp2 + hh, :]
                        vdst = vb.rearrange("p (kc hh c) -> p kc hh c",
                                            kc=16, hh=2)[:, :, hh, 0:64]
                        S_.g(lambda s=vsrc, d=vdst: nc.gpsimd.dma_start(d, s))
                        tk_v[fi] = S_.mark(("gd_v0", "gd_v1", "gd_v2")[par])

                def load_q(hp2, qb2):
                    par = (qb2 * 8 + hp2) % 2
                    S_.wait("y", tk_q_use[par])
                    if hp2 == 0 and qb2 == 0:
                        S_.wait("y", tk_qst_fin[0])
                        S_.wait("y", tk_qst_fin[1])
                    qsrc = qscr[hp2 * 128:(hp2 + 1) * 128,
                                qb2 * 512:(qb2 + 1) * 512]
                    S_.y(lambda d=qv_sb[:, par * 512:(par + 1) * 512], s=qsrc:
                         nc.sync.dma_start(d, s))
                    tk_q[par] = S_.mark("y_q0" if par == 0 else "y_q1")

                load_q(0, 0)
                load_kv(0)
                load_kv(1)
                load_kv(2)
                for qb in range(NQB):
                    for hp in range(8):
                        it = qb * 8 + hp
                        pq = it % 2
                        fbase = it * NKCG
                        if it + 1 < 8 * NQB:
                            load_q(*flat[(it + 1) * NKCG][0:2])
                        qv = qv_sb[:, pq * 512:(pq + 1) * 512]
                        tS = {}
                        sE = {}
                        tPV = {}

                        def emit_S(kc):
                            par = kc % 2
                            kpar = (fbase + kc // 16) % 3
                            ksl = kch_sb[:, kpar * 2048 + (kc % 16) * 128:
                                         kpar * 2048 + (kc % 16 + 1) * 128]
                            S_.t(lambda l=ksl[0:64, :], r=qv[0:64, :], p_=par:
                                 MM(bank(2 * p_), l, r, start=True, stop=True))
                            S_.t(lambda l=ksl[64:128, :], r=qv[64:128, :], p_=par:
                                 MM(bank(2 * p_ + 1), l, r, start=True, stop=True))
                            tS[kc] = S_.mark("t")

                        S_.wait("t", tk_q[pq])
                        S_.wait("t", tk_k[fbase])
                        S_.wait("t", tk_last_sE2)
                        emit_S(0)
                        S_.wait("t", tk_rbcc[0])
                        S_.wait("t", tk_rbcc[1])
                        S_.wait("t", tk_last_sE)
                        if KC > 16:
                            S_.wait("t", tk_k[fbase + 1])
                        emit_S(1)
                        for kc in range(KC):
                            par = kc % 2
                            fi = fbase + kc // 16
                            S_.wait("s", tS[kc])
                            S_.wait("s", tPV.get(kc - 2))
                            S_.s(lambda o=p_sb[:, par * 1024:(par + 1) * 1024],
                                 p_=par: nc.scalar.activation(
                                     o, ps[:, p_ * 1024:(p_ + 1) * 1024], AF.Exp,
                                     bias=bias_sb[:]))
                            sE[kc] = S_.mark("s")
                            kpar = fi % 3
                            vb = v65_sb[:, kpar * 2176:kpar * 2176 + 2176]
                            vs0 = vb[:, (kc % 16) * 136:(kc % 16) * 136 + 65]
                            vs1 = vb[:, (kc % 16) * 136 + 68:(kc % 16) * 136 + 133]
                            sta = (kc == 0)
                            spa = (kc == KC - 1)
                            S_.wait("t", sE[kc])
                            if kc == 0:
                                S_.wait("t", tk_oevac)
                            if kc % 16 == 0:
                                S_.wait("t", tk_v[fi])
                            S_.t(lambda l=vs0, r=p_sb[:, par * 1024:par * 1024 + 512],
                                 a=sta, b=spa:
                                 MM(ps[0:65, 2048:2560], l, r, start=a, stop=b))
                            S_.t(lambda l=vs1,
                                 r=p_sb[:, par * 1024 + 512:par * 1024 + 1024],
                                 a=sta, b=spa:
                                 MM(ps[0:65, 2560:3072], l, r, start=a, stop=b))
                            if kc + 2 < KC:
                                if (kc + 2) % 16 == 0:
                                    S_.wait("t", tk_k[fbase + (kc + 2) // 16])
                                emit_S(kc + 2)
                            tPV[kc] = S_.mark("t")
                            if kc % 16 == 15:
                                tk_k_use[fi] = tS[kc]
                                tk_v_use[fi] = tPV[kc]
                                load_kv(fi + 3)
                            pump()
                        tk_q_use[pq] = tS[KC - 1]
                        tk_last_sE = sE[KC - 1]
                        tk_last_sE2 = sE[KC - 2]
                        odst = o_sb[:, hp * T + qb * 512: hp * T + (qb + 1) * 512]
                        S_.wait("v", tPV[KC - 1])
                        S_.v(lambda d=odst[0:64, :]: nc.vector.tensor_copy(
                            d, ps[0:64, 4 * 512:5 * 512]))
                        S_.v(lambda d=odst[64:128, :]: nc.vector.tensor_copy(
                            d, ps[0:64, 5 * 512:6 * 512]))
                        tk_oevac = S_.mark("v")
                        tk_bcast_prev = None
                        tk_expj_prev = None
                        for j, head in ((0, 0), (1, 1)):
                            bnk = 4 + j
                            S_.wait("s", tPV[KC - 1])
                            S_.wait("s", tk_expj_prev)
                            S_.s(lambda b_=bnk: nc.scalar.activation(
                                rcl2_sb[:], ps[64:65, b_ * 512:(b_ + 1) * 512],
                                AF.Ln))
                            t_ = S_.mark("s")
                            S_.wait("s", t_)
                            S_.wait("s", tk_bcast_prev)
                            S_.s(lambda: nc.scalar.activation(
                                rcl_sb[:], rcl2_sb[:], AF.Exp, scale=-1.0))
                            t_ = S_.mark("s")
                            tk_expj_prev = t_
                            S_.wait("t", t_)
                            S_.wait("t", tk_rbcc[j])
                            if j == 0:
                                S_.wait("t", sE[KC - 1])
                            S_.t(lambda bb_=j: MM(
                                ps[0:128, (2 + bb_) * 512:(2 + bb_) * 512 + 512],
                                ones_f[0:1, 0:128], rcl_sb[:], start=True, stop=True))
                            t_ = S_.mark("t")
                            tk_bcast_prev = t_
                            S_.wait("v", t_)
                            S_.wait("v", tk_oev)
                            S_.v(lambda bb_=j: nc.vector.tensor_copy(
                                rbc_sb[:], ps[0:128, (2 + bb_) * 512:(2 + bb_) * 512 + 512]))
                            t_ = S_.mark("v")
                            tk_rbcc[j] = t_
                            S_.wait("v", t_)
                            dsl = odst[0:64, :] if head == 0 else odst[64:128, :]
                            rsl = rbc_sb[0:64, :] if head == 0 else rbc_sb[64:128, :]
                            S_.v(lambda d=dsl, r_=rsl: nc.vector.tensor_mul(
                                d, d, r_))
                            tk_oev = S_.mark("v")
                        if hp == 7:
                            op_state["oev"] = tk_oev
                            for tq2_ in range(4):
                                ffn_units.append(
                                    lambda g_=qb * 4 + tq2_: outproj_unit(g_))
                while ffn_i[0] < len(ffn_units):
                    ffn_units[ffn_i[0]]()
                    ffn_i[0] += 1
                tk_ffn_end_v = ffn["spk_end"]
                tk_ffn_end_t = ("t", S_.cnt["t"])
                tk_p3_end_s = tk_last_sE
                tk_p3_end_v = tk_oev

        for _sem in ("y_const", "gd_mask", "gd_wqkv", "gd_wout", "gd_wfc0",
                     "gd_wfc1", "gd_x0", "gd_x1", "gd_x2", "y_kst0", "y_kst1", "y_vst0",
                     "y_vst1", "y_kld0", "y_kld1", "gd_vld0", "gd_vld1",
                     "gd_wpj0", "gd_wpj1", "y_xtm", "y_out", "gd_wcast",
                     "y_wfc0", "y_wfc1", "y_qst0", "y_qst1", "y_k0", "y_k1",
                     "gd_v0", "gd_v1", "y_q0", "y_q1", "y_ss", "y_ss0",
                     "y_ss1", "y_out0", "y_out1", "y_k2", "gd_v2"):
            if _sem in S_.cnt:
                S_.wait("y", (_sem, S_.cnt[_sem]))

        def make_body(stream):
            def body(eng):
                last = None
                for kind, sem, val in S_.ops[stream]:
                    if kind == "op":
                        r = val()
                        if r is not None:
                            last = r
                    elif kind == "inc":
                        last.then_inc(sems[sem], val)
                    else:
                        eng.wait_ge(sems[sem], val)
            return body

        with nc.Block() as block:
            block.tensor(make_body("t"))
            block.vector(make_body("v"))
            block.scalar(make_body("s"))
            block.gpsimd(make_body("g"))
            block.sync(make_body("y"))

    return nc


def _consts():
    f = np.arange(NF)
    n = np.arange(DH)
    w = np.where((f == 0) | (f == NF - 1), 1.0, 2.0) / (DH * 8.0)
    cosb = np.cos(2 * np.pi * np.outer(f, n) / DH)
    sinb = np.sin(2 * np.pi * np.outer(f, n) / DH)
    cc = np.zeros((128, 128), np.float32)
    cs = np.zeros((128, 128), np.float32)
    cc[0:NF, 0:DH] = cosb
    cc[64:64 + NF, 64:64 + DH] = cosb
    cs[0:NF, 0:DH] = sinb
    cs[64:64 + NF, 64:64 + DH] = sinb
    wv = np.zeros((128, 1), np.float32)
    wv[0:NF, 0] = w
    wv[64:64 + NF, 0] = w
    return cc, cs, wv


def make_in_maps(x, wqkv, wout, bout, mask, wfc, wproj, n_cores=8):
    x = np.asarray(x, np.float32)
    B, S, _ = x.shape
    T = (B * S) // n_cores
    cc, cs, wv = _consts()
    wqkv = np.ascontiguousarray(np.asarray(wqkv, np.float32))
    wout = np.ascontiguousarray(np.asarray(wout, np.float32))
    bout = np.asarray(bout, np.float32).reshape(1, D)
    mask = np.asarray(mask, np.float32).reshape(NF, 1)
    wfcT = np.ascontiguousarray(np.asarray(wfc, np.float32).T)
    wprojT = np.ascontiguousarray(np.asarray(wproj, np.float32).T)
    per_seq = n_cores // B
    in_maps = []
    for c in range(n_cores):
        b, r = divmod(c, per_seq)
        xr = np.roll(x[b], -T * r, axis=0)
        in_maps.append({
            "xT": np.ascontiguousarray(xr.T),
            "x_own": np.ascontiguousarray(xr[0:T]),
            "wqkv": wqkv, "wout": wout, "bout": bout,
            "wfcT": wfcT, "wprojT": wprojT,
            "mask_l": mask, "ccos": cc, "csin": cs, "wvec": wv,
        })
    return in_maps, B, S, T


def kernel(L0_attn_x, L0_attn_w_qkv, L0_attn_w_out, L0_attn_b_out,
           L0_attn_mask_logits, L0_ffn_W_fc, L0_ffn_W_proj):
    in_maps, B, S, T = make_in_maps(
        L0_attn_x, L0_attn_w_qkv, L0_attn_w_out, L0_attn_b_out,
        L0_attn_mask_logits, L0_ffn_W_fc, L0_ffn_W_proj)
    nc = build({"S": S, "T": T})
    res = run_bass_kernel_spmd(nc, in_maps, core_ids=list(range(8)))
    per_seq = 8 // B
    out = np.empty((B, S, D), np.float32)
    for c in range(8):
        b, r = divmod(c, per_seq)
        out[b, r * T:(r + 1) * T] = np.asarray(res.results[c]["out"], np.float32)
    return out


# version 18
# speedup vs baseline: 1.2198x; 1.2198x over previous
import numpy as np
from contextlib import ExitStack

import concourse.bass as bass
import concourse.mybir as mybir
from concourse.bass_utils import run_bass_kernel_spmd

F32 = mybir.dt.float32
BF16 = mybir.dt.bfloat16
AF = mybir.ActivationFunctionType
ALU = mybir.AluOpType
AX = mybir.AxisListType

D = 1024
NF = 33
DH = 64
FF = 4096


class Sched:

    STREAMS = ("t", "v", "s", "g", "y")

    def __init__(self):
        self.ops = {k: [] for k in self.STREAMS}
        self.cnt = {}
        self._lastw = {}

    def _emit(self, stream, fn):
        self.ops[stream].append(("op", None, fn))

    def t(self, fn):
        self._emit("t", fn)

    def v(self, fn):
        self._emit("v", fn)

    def s(self, fn):
        self._emit("s", fn)

    def g(self, fn):
        self._emit("g", fn)

    def y(self, fn):
        self._emit("y", fn)

    def mark(self, sem):
        if sem.startswith("gd") or sem == "gc":
            stream = "g"
        elif sem.startswith("y"):
            stream = "y"
        else:
            stream = sem
        inc = 16 if (sem.startswith("gd") or sem.startswith("y")) else 1
        for kind_, sem_, _v in reversed(self.ops[stream]):
            if kind_ == "op":
                break
            if kind_ == "inc":
                if sem_ == sem:
                    return (sem, self.cnt[sem])
                raise RuntimeError(f"two sems on one op: {sem_} vs {sem}")
        self.cnt[sem] = self.cnt.get(sem, 0) + inc
        self.ops[stream].append(("inc", sem, inc))
        return (sem, self.cnt[sem])

    def wait(self, stream, tok):
        if tok is None:
            return
        sem, val = tok
        key = (stream, sem)
        if self._lastw.get(key, -1) >= val:
            return
        self._lastw[key] = val
        self.ops[stream].append(("wait", sem, val))


def build(cfg):
    S = cfg["S"]
    T = cfg["T"]
    NBLK = S // 512
    NOWN = T // 512
    KC = S // 128
    NQP = T // 1024
    assert NOWN * 512 == T and NQP * 1024 == T and NBLK * 512 == S

    nc = bass.Bass("TRN2", target_bir_lowering=False)

    xT = nc.dram_tensor("xT", [D, S], F32, kind="ExternalInput").ap()
    x_own = nc.dram_tensor("x_own", [T, D], F32, kind="ExternalInput").ap()
    wqkv = nc.dram_tensor("wqkv", [D, 3 * D], F32, kind="ExternalInput").ap()
    wout = nc.dram_tensor("wout", [D, D], F32, kind="ExternalInput").ap()
    bout = nc.dram_tensor("bout", [1, D], F32, kind="ExternalInput").ap()
    wfcT = nc.dram_tensor("wfcT", [D, FF], F32, kind="ExternalInput").ap()
    wprojT = nc.dram_tensor("wprojT", [FF, D], F32, kind="ExternalInput").ap()
    mask_l = nc.dram_tensor("mask_l", [NF, 1], F32, kind="ExternalInput").ap()
    ccos = nc.dram_tensor("ccos", [128, 128], F32, kind="ExternalInput").ap()
    csin = nc.dram_tensor("csin", [128, 128], F32, kind="ExternalInput").ap()
    wvec = nc.dram_tensor("wvec", [128, 1], F32, kind="ExternalInput").ap()
    out_d = nc.dram_tensor("out", [T, D], F32, kind="ExternalOutput").ap()
    kscr = nc.dram_tensor("kscr", [D, S], BF16, kind="Internal").ap()
    vscr = nc.dram_tensor("vscr", [S, D], BF16, kind="Internal").ap()
    wfcb = nc.dram_tensor("wfcb", [D, FF], BF16, kind="Internal").ap()
    qscr = nc.dram_tensor("qscr", [D, T], BF16, kind="Internal").ap()
    sscr = nc.dram_tensor("sscr", [T, D], BF16, kind="Internal").ap()
    wpjb = nc.dram_tensor("wpjb", [FF, D], BF16, kind="Internal").ap()

    S_ = Sched()
    VW = KC * 2 * 68

    with ExitStack() as ctx:
        e = ctx.enter_context
        xn_own = e(nc.sbuf_tensor("xn_own", [128, 8 * T], BF16))
        o_sb = e(nc.sbuf_tensor("o_sb", [128, 8 * T], BF16))
        bb_sb = e(nc.sbuf_tensor("bb_sb", [128, D], F32))
        bias_sb = e(nc.sbuf_tensor("bias_sb", [128, 1], F32))
        g_sb = e(nc.sbuf_tensor("g_sb", [128, 128], BF16))
        ones_bf = e(nc.sbuf_tensor("ones_bf", [128, 1], BF16))
        ones_f = e(nc.sbuf_tensor("ones_f", [1, 128], F32))
        ps = e(nc.psum_tensor("ps", [128, 4096], F32))

        def bank(b, n=1):
            return ps[:, b * 512:(b + n) * 512]

        from concourse.hw_specs import get_activation_tables

        def preload_set(name):
            idx = list(get_activation_tables(nc.m.arch).keys()).index(name)
            return nc.scalar.add_instruction(mybir.InstLoadActFuncSet(
                name=nc.get_next_instruction_name(), act_func_set_id=idx,
                ins=[], outs=[]))

        sems = {k: e(nc.semaphore("sem_" + k)) for k in (
            "t", "v", "s", "gc", "y_const", "gd_mask", "gd_wqkv", "gd_wout",
            "gd_wfc0", "gd_wfc1", "gd_x0", "gd_x1", "gd_x2", "y_kst0", "y_kst1",
            "y_vst0", "y_vst1", "y_kld0", "y_kld1", "gd_vld0", "gd_vld1",
            "gd_wpj0", "gd_wpj1", "y_xtm", "y_out", "gd_wcast",
            "y_wfc0", "y_wfc1", "y_qst0", "y_qst1", "y_k0", "y_k1",
            "gd_v0", "gd_v1", "y_q0", "y_q1", "y_ss", "y_ss0", "y_ss1",
            "y_out0", "y_out1", "y_k2", "gd_v2")}
        MM = nc.tensor.matmul

        with ExitStack() as qp_ctx:
            qst_sb = qp_ctx.enter_context(nc.sbuf_tensor("qst_sb", [128, 2 * 512], BF16))

            with ExitStack() as c1:
                e1 = c1.enter_context
                wqkv_sb = e1(nc.sbuf_tensor("wqkv_sb", [128, 8 * 3 * D], BF16))
                brow = e1(nc.sbuf_tensor("brow", [1, D], F32))
                x_sb = e1(nc.sbuf_tensor("x_sb", [128, 3 * 4096], BF16))
                xsq_sb = e1(nc.sbuf_tensor("xsq_sb", [128, 2 * 512], BF16))
                xnt_sb = e1(nc.sbuf_tensor("xnt_sb", [128, 1024], BF16))
                kst_sb = e1(nc.sbuf_tensor("kst_sb", [128, 2 * 512], BF16))
                vst_sb = e1(nc.sbuf_tensor("vst_sb", [128, 2 * 1024], BF16))
                qt_sb = e1(nc.sbuf_tensor("qt_sb", [128, 512], BF16))
                sq_sb = e1(nc.sbuf_tensor("sq_sb", [128, 2 * 512], BF16))
                mb_sb = e1(nc.sbuf_tensor("mb_sb", [128, 1024], BF16))
                rb_sb = e1(nc.sbuf_tensor("rb_sb", [128, 1024], F32))
                mrow = e1(nc.sbuf_tensor("mrow", [1, 512], F32))
                r1 = e1(nc.sbuf_tensor("r1", [1, 512], F32))
                r2 = e1(nc.sbuf_tensor("r2", [1, 512], F32))
                r3 = e1(nc.sbuf_tensor("r3", [1, 512], F32))
                s2_sb = e1(nc.sbuf_tensor("s2_sb", [128, 1], F32))
                cs_sb = e1(nc.sbuf_tensor("cs_sb", [128, 128], F32))
                ss_sb = e1(nc.sbuf_tensor("ss_sb", [128, 128], F32))
                ccos_sb = e1(nc.sbuf_tensor("ccos_sb", [128, 128], F32))
                csin_sb = e1(nc.sbuf_tensor("csin_sb", [128, 128], F32))
                wvec_sb = e1(nc.sbuf_tensor("wvec_sb", [128, 1], F32))
                mk2_sb = e1(nc.sbuf_tensor("mk2_sb", [1, 1], F32))
                mq2_sb = e1(nc.sbuf_tensor("mq2_sb", [1, 1], F32))
                t11 = e1(nc.sbuf_tensor("t11", [1, 8], F32))
                eps_sb = e1(nc.sbuf_tensor("eps_sb", [1, 1], F32))

                S_.y(lambda: nc.sync.dma_start(ccos_sb[:], ccos))
                S_.mark("y_const")
                S_.y(lambda: nc.sync.dma_start(csin_sb[:], csin))
                S_.mark("y_const")
                S_.y(lambda: nc.sync.dma_start(wvec_sb[:], wvec))
                S_.mark("y_const")
                S_.y(lambda: nc.sync.dma_start(brow[:], bout))
                tk_consts = S_.mark("y_const")

                S_.v(lambda: nc.vector.memset(s2_sb[:], 0.0))
                tk_ms = S_.mark("v")
                S_.wait("g", tk_ms)
                S_.g(lambda: nc.gpsimd.dma_start(s2_sb[0:NF, :], mask_l))
                S_.mark("gd_mask")
                S_.g(lambda: nc.gpsimd.dma_start(s2_sb[64:64 + NF, :], mask_l))
                tk_mask = S_.mark("gd_mask")

                S_.v(lambda: nc.vector.memset(ones_bf[:], 1.0))
                S_.v(lambda: nc.vector.memset(ones_f[:], 1.0))
                S_.v(lambda: nc.vector.memset(mk2_sb[:], 0.0))
                S_.v(lambda: nc.vector.memset(mq2_sb[:], 0.0))
                S_.v(lambda: nc.vector.memset(eps_sb[:], 1e-5))
                tk_ones = S_.mark("v")

                S_.s(lambda: nc.scalar.preload_act_table(AF.Sigmoid))
                S_.wait("s", tk_mask)
                S_.s(lambda: nc.scalar.activation(s2_sb[:], s2_sb[:], AF.Sigmoid))
                tk_sg = S_.mark("s")
                S_.wait("v", tk_sg)
                S_.v(lambda: nc.vector.tensor_mul(s2_sb[:], s2_sb[:], s2_sb[:]))
                t_ = S_.mark("v")
                S_.wait("v", t_)
                S_.wait("v", tk_consts)
                S_.v(lambda: nc.vector.tensor_scalar_mul(s2_sb[:], s2_sb[:], wvec_sb[:]))
                t_ = S_.mark("v")
                S_.wait("v", t_)
                S_.v(lambda: nc.vector.tensor_scalar_mul(cs_sb[:], ccos_sb[:], s2_sb[:]))
                S_.v(lambda: nc.vector.tensor_scalar_mul(ss_sb[:], csin_sb[:], s2_sb[:]))
                tk_cs = S_.mark("v")
                S_.wait("t", tk_cs)
                S_.t(lambda: MM(ps[0:128, 0:128], cs_sb[:], ccos_sb[:],
                                start=True, stop=False))
                S_.t(lambda: MM(ps[0:128, 0:128], ss_sb[:], csin_sb[:],
                                start=False, stop=True))
                tk_gmm = S_.mark("t")
                S_.wait("v", tk_gmm)
                S_.v(lambda: nc.vector.tensor_copy(g_sb[:], ps[0:128, 0:128]))
                tk_g = S_.mark("v")
                S_.wait("t", tk_g)
                S_.t(lambda: MM(bank(0), ones_f[:], brow[0:1, 0:512],
                                start=True, stop=True))
                S_.t(lambda: MM(bank(1), ones_f[:], brow[0:1, 512:1024],
                                start=True, stop=True))
                tk_bbm = S_.mark("t")
                S_.wait("v", tk_bbm)
                S_.v(lambda: nc.vector.tensor_copy(bb_sb[:, 0:512], bank(0)))
                S_.v(lambda: nc.vector.tensor_copy(bb_sb[:, 512:1024], bank(1)))
                tk_bb = S_.mark("v")

                S_.s(lambda: preload_set("natural_log_exp_and_others"))

                tok_xn = [None] * NBLK
                tk_kst_dma = [None, None]
                tk_qst_dma = [None, None]
                tk_vst_dma = [None, None]
                prev_m1 = None
                prev_lnmm = None
                tk_mbc_last = None
                tk_kn2v = None
                tk_qn2v = None
                prev_kev = None
                prev_vev = None
                prev_ksq_mm = None
                prev_qg = None
                prev_qpev = None
                tk_blk_t = [None] * NBLK

                xv_save = {}

                def ln_stats(blk):
                    nonlocal prev_m1
                    par = blk % 3
                    xv = x_sb[:, par * 4096:par * 4096 + 4096]
                    xv_save[blk] = xv
                    srcx = xT.rearrange("(dc p) s -> p dc s", p=128)[
                        :, :, blk * 512:(blk + 1) * 512]
                    dstx = xv.rearrange("p (dc s) -> p dc s", dc=8)
                    if blk >= 3:
                        S_.wait("g", tok_xn[blk - 3])
                        S_.wait("g", tk_blk_t[blk - 3])
                    S_.g(lambda s=srcx, d=dstx: nc.gpsimd.dma_start(d, s))
                    tk_x = S_.mark(("gd_x0", "gd_x1", "gd_x2")[par])

                    S_.wait("t", tk_x)
                    S_.wait("t", tk_ones)
                    if blk == 0:
                        S_.wait("t", tk_bb)
                    if prev_m1 is not None:
                        S_.wait("t", prev_m1)
                    if tk_mbc_last is not None:
                        S_.wait("t", tk_mbc_last)
                    for dc in range(8):
                        S_.t(lambda r=xv[:, dc * 512:(dc + 1) * 512],
                             st=(dc == 0), sp=(dc == 7):
                             MM(bank(0)[0:1, :], ones_bf[:], r, start=st, stop=sp))
                    tk_sx = S_.mark("t")
                    tk_xm = {}
                    tk_x2mm = {}
                    for dc in range(8):
                        xq = xsq_sb[:, (dc % 2) * 512:(dc % 2) * 512 + 512]
                        if dc >= 2:
                            S_.wait("v", tk_x2mm[dc - 2])
                        S_.wait("v", tk_x)
                        S_.v(lambda a=xq, b=xv[:, dc * 512:(dc + 1) * 512]:
                             nc.vector.tensor_mul(a, b, b))
                        tk_xm[dc] = S_.mark("v")
                        if dc >= 1:
                            S_.wait("t", tk_xm[dc - 1])
                            S_.t(lambda r=xsq_sb[:, ((dc - 1) % 2) * 512:
                                              ((dc - 1) % 2) * 512 + 512],
                                 st=(dc - 1 == 0), sp=False:
                                 MM(bank(1)[0:1, :], ones_bf[:], r, start=st, stop=sp))
                            tk_x2mm[dc - 1] = S_.mark("t")
                    S_.wait("t", tk_xm[7])
                    S_.t(lambda r=xsq_sb[:, (7 % 2) * 512:(7 % 2) * 512 + 512]:
                         MM(bank(1)[0:1, :], ones_bf[:], r, start=False, stop=True))
                    tk_x2mm[7] = S_.mark("t")
                    if prev_lnmm is not None:
                        S_.wait("v", prev_lnmm)
                    S_.wait("v", tk_sx)
                    S_.wait("v", tk_x2mm[7])
                    S_.v(lambda: nc.vector.tensor_scalar_mul(mrow[:], bank(0)[0:1, :], 1.0 / D))
                    S_.v(lambda: nc.vector.tensor_scalar_mul(r1[:], bank(1)[0:1, :], 1.0 / D))
                    prev_m1 = S_.mark("v")
                    S_.wait("v", prev_m1)
                    S_.v(lambda: nc.vector.tensor_mul(r2[:], mrow[:], mrow[:]))
                    t_ = S_.mark("v")
                    S_.wait("v", t_)
                    S_.v(lambda: nc.vector.tensor_sub(r1[:], r1[:], r2[:]))
                    tk_var = S_.mark("v")
                    S_.wait("s", tk_var)
                    S_.wait("s", tk_ones)
                    S_.s(lambda: nc.scalar.activation(r2[:], r1[:], AF.Ln, bias=eps_sb[:]))
                    t_ = S_.mark("s")
                    S_.wait("s", t_)
                    S_.s(lambda: nc.scalar.activation(r3[:], r2[:], AF.Exp, scale=-0.5))
                    tk_rstd_a[blk] = S_.mark("s")

                pending_xn = []

                def ln_apply(blk):
                    nonlocal prev_lnmm, tk_mbc_last
                    xv = xv_save[blk]
                    S_.wait("t", tk_rstd_a[blk])
                    if prev_m1 is not None:
                        S_.wait("t", prev_m1)
                    S_.t(lambda: MM(bank(0), ones_f[:], mrow[:], start=True, stop=True))
                    S_.t(lambda: MM(bank(1), ones_f[:], r3[:], start=True, stop=True))
                    prev_lnmm = S_.mark("t")
                    S_.wait("s", prev_lnmm)
                    if blk > 0:
                        S_.wait("s", tok_xn[blk - 1])
                    S_.s(lambda: nc.scalar.activation(mb_sb[:, 0:512], bank(0), AF.Copy))
                    S_.s(lambda: nc.scalar.activation(mb_sb[:, 512:1024], bank(0), AF.Copy))
                    S_.s(lambda: nc.scalar.activation(rb_sb[:, 0:512], bank(1), AF.Copy))
                    S_.s(lambda: nc.scalar.activation(rb_sb[:, 512:1024], bank(1), AF.Copy))
                    tk_mbc = S_.mark("s")
                    tk_mbc_last = tk_mbc
                    if blk < NOWN:
                        xn_dst = xn_own[:, blk * 4096:(blk + 1) * 4096]
                    else:
                        xn_dst = xv
                    xn_save[blk] = xn_dst
                    pending_xn.append(lambda: ln_apply_xn(blk, xv, xn_dst, tk_mbc))

                def ln_apply_xn(blk, xv, xn_dst, tk_mbc):
                    S_.wait("v", tk_mbc)
                    for dc2 in range(4):
                        xc = xv[:, dc2 * 1024:(dc2 + 1) * 1024]
                        xo = xn_dst[:, dc2 * 1024:(dc2 + 1) * 1024]
                        S_.v(lambda a=xnt_sb[:], b=xc:
                             nc.vector.tensor_sub(a, b, mb_sb[:]))
                        t_ = S_.mark("v")
                        S_.wait("v", t_)
                        S_.v(lambda a=xo, b=xnt_sb[:]:
                             nc.vector.tensor_mul(a, b, rb_sb[:]))
                        t_ = S_.mark("v")
                        S_.wait("v", t_)
                    tok_xn[blk] = t_

                def stage_kvq(blk):
                    nonlocal tk_kn2v, tk_qn2v, prev_kev, prev_vev, prev_ksq_mm, prev_qg, prev_qpev
                    par = blk % 2
                    xn_src = xn_save[blk]
                    tk_ksq = {}
                    S_.wait("t", tok_xn[blk])
                    S_.wait("t", tk_wqkv)
                    if tk_kn2v is not None:
                        S_.wait("t", tk_kn2v)
                    if tk_qn2v is not None:
                        S_.wait("t", tk_qn2v)
                    if prev_qpev is not None:
                        S_.wait("t", prev_qpev)
                    kev_par = getattr(S_, "_kev_par", [None, None, None, None])
                    S_._kev_par = kev_par
                    for dc in range(8):
                        kbk = 4 + dc % 4
                        kdst = kst_sb[:, (dc % 2) * 512:(dc % 2) * 512 + 512]
                        if kev_par[dc % 4] is not None:
                            S_.wait("t", kev_par[dc % 4])
                        for di in range(8):
                            lhs = wqkv_sb[:, di * 3 * D + D + dc * 128:
                                          di * 3 * D + D + (dc + 1) * 128]
                            rhs = xn_src[:, di * 512:(di + 1) * 512]
                            S_.t(lambda l=lhs, r=rhs, b=kbk, st=(di == 0), sp=(di == 7):
                                 MM(bank(b), l, r, start=st, stop=sp))
                        tk_kmm = S_.mark("t")
                        S_.wait("s", tk_kmm)
                        S_.wait("s", tk_kst_dma[dc % 2])
                        S_.s(lambda d=kdst, b=kbk: nc.scalar.activation(d, bank(b), AF.Copy))
                        prev_kev = S_.mark("s")
                        kev_par[dc % 4] = prev_kev
                        if prev_ksq_mm is not None:
                            S_.wait("v", prev_ksq_mm)
                        S_.wait("v", prev_kev)
                        S_.v(lambda d=kdst, q=sq_sb[:, (dc % 2) * 512:(dc % 2) * 512 + 512]:
                             nc.vector.tensor_mul(q, d, d))
                        tk_ksq[dc] = S_.mark("v")
                        if dc >= 1:
                            S_.wait("t", tk_ksq[dc - 1])
                            S_.t(lambda q=sq_sb[:, ((dc - 1) % 2) * 512:
                                              ((dc - 1) % 2) * 512 + 512],
                                 st=(dc - 1 == 0):
                                 MM(bank(2)[0:1, :], ones_bf[:], q, start=st, stop=False))
                            prev_ksq_mm = S_.mark("t")
                        S_.wait("y", prev_kev)
                        kdd = kscr.rearrange("(dc p) s -> dc p s", p=128)[dc][
                            :, blk * 512:(blk + 1) * 512]
                        S_.y(lambda dd=kdd, sd=kdst: nc.sync.dma_start(dd, sd))
                        tk_kst_dma[dc % 2] = S_.mark("y_kst0" if dc % 2 == 0 else "y_kst1")
                    S_.wait("t", tk_ksq[7])
                    S_.t(lambda q=sq_sb[:, (7 % 2) * 512:(7 % 2) * 512 + 512]:
                         MM(bank(2)[0:1, :], ones_bf[:], q, start=False, stop=True))
                    prev_ksq_mm = S_.mark("t")
                    if tk_kn2v is not None:
                        S_.wait("v", tk_kn2v)
                    S_.wait("v", prev_ksq_mm)
                    S_.v(lambda: nc.vector.reduce_max(t11[0:1, 0:1], bank(2)[0:1, :], axis=AX.X))
                    t_ = S_.mark("v")
                    S_.wait("v", t_)
                    S_.v(lambda: nc.vector.tensor_max(mk2_sb[:], mk2_sb[:], t11[0:1, 0:1]))
                    tk_kn2v = S_.mark("v")

                    vev_par = kev_par
                    for tq in range(4):
                        vdst = vst_sb[:, (tq % 2) * 1024:(tq % 2) * 1024 + 1024]
                        for nn in range(2):
                            gidx = (tq * 2 + nn) % 4
                            vbk = 4 + gidx
                            if vev_par[gidx] is not None:
                                S_.wait("t", vev_par[gidx])
                            for di in range(8):
                                lhs = xn_src[:, di * 512 + tq * 128: di * 512 + (tq + 1) * 128]
                                rh = wqkv_sb[:, di * 3 * D + 2 * D + nn * 512:
                                             di * 3 * D + 2 * D + (nn + 1) * 512]
                                S_.t(lambda l=lhs, r=rh, b=vbk, st=(di == 0), sp=(di == 7):
                                     MM(bank(b), l, r, start=st, stop=sp))
                            tk_vmm = S_.mark("t")
                            S_.wait("s", tk_vmm)
                            S_.wait("s", tk_vst_dma[tq % 2])
                            S_.s(lambda d=vdst, n=nn, b=vbk:
                                 nc.scalar.activation(d[:, n * 512:(n + 1) * 512],
                                                      bank(b), AF.Copy))
                            prev_vev = S_.mark("s")
                            vev_par[gidx] = prev_vev
                        S_.wait("y", prev_vev)
                        vdd = vscr[blk * 512 + tq * 128: blk * 512 + (tq + 1) * 128, :]
                        S_.y(lambda dd=vdd, sd=vdst: nc.sync.dma_start(dd, sd))
                        tk_vst_dma[tq % 2] = S_.mark("y_vst0" if tq % 2 == 0 else "y_vst1")
                        if tq == 1 and pending_xn:
                            pending_xn.pop(0)()

                    if blk < NOWN:
                        tk_qn2_mm = None
                        tk_qsq = {}
                        for hp in range(8):
                            if prev_qg is not None:
                                S_.wait("t", prev_qg)
                            S_.wait("t", kev_par[3])
                            for di in range(8):
                                lhs = wqkv_sb[:, di * 3 * D + hp * 128:
                                              di * 3 * D + (hp + 1) * 128]
                                rhs = xn_src[:, di * 512:(di + 1) * 512]
                                S_.t(lambda l=lhs, r=rhs, st=(di == 0), sp=(di == 7):
                                     MM(bank(7), l, r, start=st, stop=sp))
                            tk_qmm = S_.mark("t")
                            S_.wait("s", tk_qmm)
                            if prev_qg is not None:
                                S_.wait("s", prev_qg)
                            S_.s(lambda: nc.scalar.activation(qt_sb[:], bank(7), AF.Copy))
                            tk_qev = S_.mark("s")
                            S_.wait("t", tk_qev)
                            S_.wait("t", tk_g)
                            S_.wait("t", kev_par[0])
                            if prev_qpev is not None:
                                S_.wait("t", prev_qpev)
                            S_.t(lambda: MM(bank(4), g_sb[:], qt_sb[:],
                                            start=True, stop=True))
                            prev_qg = S_.mark("t")
                            S_.wait("s", prev_qg)
                            qdst = qst_sb[:, (hp % 2) * 512:(hp % 2) * 512 + 512]
                            S_.wait("s", tk_qst_dma[hp % 2])
                            S_.s(lambda d=qdst: nc.scalar.activation(d, bank(4), AF.Copy))
                            prev_qpev = S_.mark("s")
                            S_.wait("y", prev_qpev)
                            qdd = qscr[hp * 128:(hp + 1) * 128,
                                       blk * 512:(blk + 1) * 512]
                            S_.y(lambda dd=qdd, sd=qdst: nc.sync.dma_start(dd, sd))
                            tk_qst_dma[hp % 2] = S_.mark(
                                "y_qst0" if hp % 2 == 0 else "y_qst1")
                            if prev_ksq_mm is not None:
                                S_.wait("v", prev_ksq_mm)
                            if tk_qn2_mm is not None:
                                S_.wait("v", tk_qn2_mm)
                            S_.wait("v", prev_qpev)
                            S_.v(lambda d=qdst, q=sq_sb[:, (hp % 2) * 512:
                                              (hp % 2) * 512 + 512]:
                                 nc.vector.tensor_mul(q, d, d))
                            tk_qsq[hp] = S_.mark("v")
                            if hp >= 1:
                                S_.wait("t", tk_qsq[hp - 1])
                                S_.t(lambda q=sq_sb[:, ((hp - 1) % 2) * 512:
                                                  ((hp - 1) % 2) * 512 + 512],
                                     st=(hp - 1 == 0):
                                     MM(bank(3)[0:1, :], ones_bf[:], q,
                                        start=st, stop=False))
                                tk_qn2_mm = S_.mark("t")
                                prev_ksq_mm = tk_qn2_mm
                        S_.wait("t", tk_qsq[7])
                        S_.t(lambda q=sq_sb[:, (7 % 2) * 512:(7 % 2) * 512 + 512]:
                             MM(bank(3)[0:1, :], ones_bf[:], q, start=False, stop=True))
                        tk_qn2_mm = S_.mark("t")
                        prev_ksq_mm = tk_qn2_mm
                        S_.wait("v", tk_qn2_mm)
                        S_.v(lambda: nc.vector.reduce_max(t11[0:1, 1:2], bank(3)[0:1, :],
                                                          axis=AX.X))
                        t_ = S_.mark("v")
                        S_.wait("v", t_)
                        S_.v(lambda: nc.vector.tensor_max(mq2_sb[:], mq2_sb[:], t11[0:1, 1:2]))
                        tk_qn2v = S_.mark("v")
                    tk_blk_t[blk] = S_.mark("t")

                xn_save = {}
                tk_rstd_a = {}
                ln_stats(0)
                for dc in range(8):
                    srcw = wqkv.rearrange("(dc p) n -> dc p n", p=128)[dc]
                    dstw = wqkv_sb[:, dc * 3 * D:(dc + 1) * 3 * D]
                    S_.g(lambda s=srcw, d=dstw: nc.gpsimd.dma_start(d, s))
                    tk_wqkv = S_.mark("gd_wqkv")
                ln_apply(0)
                pending_xn.pop(0)()
                if NBLK > 1:
                    ln_stats(1)
                for ch in range(8):
                    S_.g(lambda c=ch: nc.gpsimd.dma_start(
                        wfcb[c * 128:(c + 1) * 128, :], wfcT[c * 128:(c + 1) * 128, :]))
                    tk_wcast = S_.mark("gd_wcast")
                for ch in range(8):
                    S_.g(lambda c=ch: nc.gpsimd.dma_start(
                        wpjb[c * 512:(c + 1) * 512, :], wprojT[c * 512:(c + 1) * 512, :]))
                    tk_wcast = S_.mark("gd_wcast")
                for blk in range(NBLK):
                    if blk + 1 < NBLK:
                        ln_apply(blk + 1)
                    stage_kvq(blk)
                    if blk + 2 < NBLK:
                        ln_stats(blk + 2)

                tk_qst_fin = [("y_qst0", S_.cnt.get("y_qst0", 0)),
                              ("y_qst1", S_.cnt.get("y_qst1", 0))]
                tk_kst_fin = [("y_kst0", S_.cnt.get("y_kst0", 0)),
                              ("y_kst1", S_.cnt.get("y_kst1", 0))]
                tk_vst_fin = [("y_vst0", S_.cnt.get("y_vst0", 0)),
                              ("y_vst1", S_.cnt.get("y_vst1", 0))]
                S_.wait("v", tk_kn2v)
                S_.wait("v", tk_qn2v)
                S_.v(lambda: nc.vector.tensor_mul(t11[0:1, 2:3], mq2_sb[:], mk2_sb[:]))
                t_ = S_.mark("v")
                S_.wait("s", t_)
                S_.s(lambda: nc.scalar.activation(t11[0:1, 4:5], t11[0:1, 2:3], AF.Ln))
                t_ = S_.mark("s")
                S_.wait("s", t_)
                S_.s(lambda: nc.scalar.activation(t11[0:1, 3:4], t11[0:1, 4:5], AF.Exp,
                                                  scale=0.5))
                tk_M = S_.mark("s")
                S_.wait("t", tk_M)
                S_.wait("t", tk_kn2v)
                S_.t(lambda: MM(ps[0:128, 0:1], ones_f[:], t11[0:1, 3:4],
                                start=True, stop=True))
                t_ = S_.mark("t")
                S_.wait("v", t_)
                S_.v(lambda: nc.vector.tensor_scalar_mul(bias_sb[:], ps[0:128, 0:1], -1.0))
                tk_bias = S_.mark("v")

            tk_p1_t = ("t", S_.cnt["t"])
            with ExitStack() as c3:
                e3 = c3.enter_context
                kch_sb = e3(nc.sbuf_tensor("kch_sb", [128, 3 * 2048], BF16))
                v65_sb = e3(nc.sbuf_tensor("v65_sb", [128, 3 * 16 * 136], BF16))
                qv_sb = e3(nc.sbuf_tensor("qv_sb", [128, 2 * 512], BF16))
                p_sb = e3(nc.sbuf_tensor("p_sb", [128, 2 * 1024], BF16))
                rcl_sb = e3(nc.sbuf_tensor("rcl_sb", [1, 512], F32))
                rcl2_sb = e3(nc.sbuf_tensor("rcl2_sb", [1, 512], F32))
                rbc_sb = e3(nc.sbuf_tensor("rbc_sb", [128, 512], F32))
                wfc_sb = e3(nc.sbuf_tensor("wfc_sb", [128, 2 * 8 * 512], BF16))
                wpj_sb = e3(nc.sbuf_tensor("wpj_sb", [128, 2 * 2 * 1024], BF16))
                wout_sb = e3(nc.sbuf_tensor("wout_sb", [128, 8 * D], BF16))
                ss_ld = e3(nc.sbuf_tensor("ss_ld", [128, 1024], BF16))
                x_tm = e3(nc.sbuf_tensor("x_tm", [128, D], F32))
                eo_sb = e3(nc.sbuf_tensor("eo_sb", [128, D], F32))
                h_sb = e3(nc.sbuf_tensor("h_sb", [128, 32 * 512], BF16))
                ssb_st = e3(nc.sbuf_tensor("ssb_st", [128, 1024], BF16))
                y_sb = e3(nc.sbuf_tensor("y_sb", [128, D], F32))
                mem_sb = e3(nc.sbuf_tensor("mem_sb", [128, D], F32))
                spk_sb = e3(nc.sbuf_tensor("spk_sb", [128, D], F32))
                ssum_sb = e3(nc.sbuf_tensor("ssum_sb", [128, D], F32))

                S_.wait("g", tk_bias)
                for dc in range(8):
                    srcw = wout.rearrange("(dc p) n -> dc p n", p=128)[dc]
                    dstw = wout_sb[:, dc * D:(dc + 1) * D]
                    S_.g(lambda s=srcw, d=dstw: nc.gpsimd.dma_start(d, s))
                    tk_wout = S_.mark("gd_wout")

                ffn = {"wfc": [None, None], "wfc_use": [None, None],
                       "wpj": [None, None], "wpj_use": [None, None],
                       "hev": [None, None], "sq_last": None,
                       "yev_last": None, "hall": None, "ybank": None,
                       "spk_end": None}

                def load_wfc(hf):
                    if hf >= NOWN * 8:
                        return
                    fb2, g = divmod(hf, 8)
                    par = hf % 2
                    S_.wait("y", ffn["wfc_use"][par])
                    S_.wait("y", tk_wcast)
                    if hf <= 1:
                        S_.wait("y", tk_bias)
                    srcf = wfcb.rearrange("(dc p) n -> p dc n", p=128)[
                        :, :, g * 512:(g + 1) * 512]
                    dstf = wfc_sb[:, par * 4096:(par + 1) * 4096].rearrange(
                        "p (dc c) -> p dc c", dc=8)
                    S_.y(lambda s=srcf, d=dstf: nc.sync.dma_start(d, s))
                    ffn["wfc"][par] = S_.mark("y_wfc0" if par == 0 else "y_wfc1")

                def load_wpj(pf):
                    if pf >= NOWN * 4 * 16:
                        return
                    g = pf % 16
                    par = pf % 2
                    S_.wait("g", ffn["wpj_use"][par])
                    S_.wait("g", tk_wcast)
                    S_.wait("g", tk_bias)
                    srcp = wpjb.rearrange("(f p) n -> p f n", p=128)[
                        :, g * 2:(g + 1) * 2, :]
                    dstp = wpj_sb[:, par * 2048:(par + 1) * 2048].rearrange(
                        "p (f c) -> p f c", f=2)
                    S_.g(lambda s=srcp, d=dstp: nc.gpsimd.dma_start(d, s))
                    ffn["wpj"][par] = S_.mark("gd_wpj0" if par == 0 else "gd_wpj1")

                def h_unit(fb, fc, half):
                    hf = fb * 8 + fc // 4
                    par = hf % 2
                    hb = 6 + (fc % 2)
                    if hf == 0 and fc == 0 and half == 0:
                        load_wfc(0)
                        load_wfc(1)
                    S_.wait("t", ffn["wfc"][par])
                    if half == 0:
                        S_.wait("t", ffn["hev"][fc % 2])
                    if fb == 0 and fc == 0:
                        S_.wait("t", tk_bias)
                    for di in range(half * 4, half * 4 + 4):
                        lhs = wfc_sb[:, par * 4096 + di * 512 + (fc % 4) * 128:
                                     par * 4096 + di * 512 + (fc % 4 + 1) * 128]
                        rhs = xn_own[:, fb * 4096 + di * 512:
                                     fb * 4096 + (di + 1) * 512]
                        S_.t(lambda l=lhs, r=rhs, b=hb, st=(di == 0), sp=(di == 7):
                             MM(bank(b), l, r, start=st, stop=sp))
                    tk_hmm = S_.mark("t")
                    if half == 0:
                        return
                    if fc % 4 == 3:
                        ffn["wfc_use"][par] = tk_hmm
                        load_wfc(hf + 2)
                    S_.wait("v", tk_hmm)
                    S_.wait("v", ffn["ybank"])
                    S_.wait("v", ffn["sq_last"])
                    hro = h_sb[:, fc * 512:(fc + 1) * 512]
                    S_.v(lambda o=hro, b=hb: nc.vector.tensor_scalar(
                        o, bank(b), 0.0, None, ALU.max))
                    t_ = S_.mark("v")
                    ffn["hev"][fc % 2] = t_
                    S_.wait("v", t_)
                    S_.v(lambda o=hro: nc.vector.tensor_mul(o, o, o))
                    ffn["sq_last"] = S_.mark("v")
                    if fc == 31:
                        ffn["hall"] = ffn["sq_last"]

                def y_unit(fb, tq, g, half):
                    pf = (fb * 4 + tq) * 16 + g
                    par = pf % 2
                    if pf == 0 and half == 0:
                        load_wpj(0)
                        load_wpj(1)
                    S_.wait("t", ffn["wpj"][par])
                    S_.wait("t", ffn["hall"])
                    if g == 0 and half == 0:
                        S_.wait("t", ffn["ybank"])
                    for f2 in (half,):
                        f = g * 2 + f2
                        lhs = h_sb[:, f * 512 + tq * 128: f * 512 + (tq + 1) * 128]
                        for nn in range(2):
                            rhs = wpj_sb[:, par * 2048 + f2 * 1024 + nn * 512:
                                         par * 2048 + f2 * 1024 + (nn + 1) * 512]
                            b = 6 + nn
                            S_.t(lambda l=lhs, r=rhs, b_=b,
                                 st=(f == 0), sp=(f == 31):
                                 MM(bank(b_), l, r, start=st, stop=sp))
                    tk_ymm = S_.mark("t")
                    if half == 0:
                        return
                    ffn["wpj_use"][par] = tk_ymm
                    load_wpj(pf + 2)
                    if g == 15:
                        gtq = fb * 4 + tq
                        S_.wait("v", tk_ymm)
                        S_.wait("v", ffn["spk_end"])
                        S_.v(lambda: nc.vector.tensor_copy(y_sb[:, 0:512], bank(6)))
                        S_.v(lambda: nc.vector.tensor_copy(y_sb[:, 512:1024], bank(7)))
                        t_ = S_.mark("v")
                        ffn["ybank"] = t_
                        S_.wait("v", t_)
                        S_.v(lambda: nc.vector.tensor_scalar(
                            ssum_sb[:], y_sb[:], 1.0, None, ALU.is_gt))
                        t_ = S_.mark("v")
                        S_.wait("v", t_)
                        S_.v(lambda: nc.vector.scalar_tensor_tensor(
                            mem_sb[:], y_sb[:], 1.9, ssum_sb[:], ALU.mult,
                            ALU.subtract))
                        t_ = S_.mark("v")
                        S_.wait("v", t_)
                        S_.v(lambda: nc.vector.tensor_scalar(
                            spk_sb[:], mem_sb[:], 1.0, None, ALU.is_gt))
                        t_ = S_.mark("v")
                        S_.wait("v", t_)
                        S_.v(lambda: nc.vector.tensor_add(ssum_sb[:], ssum_sb[:],
                                                          spk_sb[:]))
                        t_ = S_.mark("v")
                        for stp in range(3):
                            S_.wait("v", t_)
                            S_.v(lambda: nc.vector.scalar_tensor_tensor(
                                y_sb[:] if False else spk_sb[:], mem_sb[:], 0.9,
                                y_sb[:], ALU.mult, ALU.add) if False else
                                nc.vector.scalar_tensor_tensor(
                                    mem_sb[:], mem_sb[:], 0.9, y_sb[:],
                                    ALU.mult, ALU.add))
                            t_ = S_.mark("v")
                            S_.wait("v", t_)
                            S_.v(lambda: nc.vector.tensor_sub(mem_sb[:], mem_sb[:],
                                                              spk_sb[:]))
                            t_ = S_.mark("v")
                            S_.wait("v", t_)
                            S_.v(lambda: nc.vector.tensor_scalar(
                                spk_sb[:], mem_sb[:], 1.0, None, ALU.is_gt))
                            t_ = S_.mark("v")
                            S_.wait("v", t_)
                            S_.v(lambda: nc.vector.tensor_add(ssum_sb[:], ssum_sb[:],
                                                              spk_sb[:]))
                            t_ = S_.mark("v")
                        S_.wait("v", t_)
                        S_.wait("v", ffn.get("ss_dma"))
                        S_.v(lambda: nc.vector.tensor_copy(ssb_st[:], ssum_sb[:]))
                        t_ = S_.mark("v")
                        ffn["spk_end"] = t_
                        S_.wait("y", t_)
                        S_.y(lambda g_=gtq: nc.sync.dma_start(
                            sscr[g_ * 128:(g_ + 1) * 128, :], ssb_st[:]))
                        ffn["ss_dma"] = S_.mark("y_ss")

                def h_full(fb, fc):
                    h_unit(fb, fc, 0)
                    h_unit(fb, fc, 1)

                def y_full(fb, tq, g):
                    y_unit(fb, tq, g, 0)
                    y_unit(fb, tq, g, 1)

                op_state = {"edone": None, "xo": None, "ssl": None,
                            "out": [None, None], "oev": None}

                def outproj_unit(gtq):
                    S_.wait("t", ffn["hev"][0])
                    S_.wait("t", ffn["hev"][1])
                    S_.wait("t", ffn["ybank"])
                    S_.wait("t", tk_wout)
                    S_.wait("t", op_state["oev"])
                    S_.wait("t", tk_bb)
                    for hp2 in range(8):
                        lhs = o_sb[:, hp2 * T + gtq * 128: hp2 * T + (gtq + 1) * 128]
                        for nn in range(2):
                            rhs = wout_sb[:, hp2 * D + nn * 512:
                                          hp2 * D + (nn + 1) * 512]
                            S_.t(lambda l=lhs, r=rhs, b=6 + nn,
                                 st=(hp2 == 0), sp=(hp2 == 7):
                                 MM(bank(b), l, r, start=st, stop=sp))
                    tk_at = S_.mark("t")
                    S_.wait("y", op_state["edone"])
                    S_.wait("y", ffn["ss_dma"])
                    S_.y(lambda g_=gtq: nc.sync.dma_start(
                        x_tm[:], x_own[g_ * 128:(g_ + 1) * 128, :]))
                    tk_xo = S_.mark("y_xtm")
                    S_.y(lambda g_=gtq: nc.sync.dma_start(
                        ss_ld[:], sscr[g_ * 128:(g_ + 1) * 128, :]))
                    tk_ssl = S_.mark("y_ss0")
                    S_.wait("v", op_state["out"][0])
                    S_.wait("v", op_state["out"][1])
                    S_.wait("v", tk_at)
                    S_.wait("v", tk_bb)
                    S_.v(lambda: nc.vector.tensor_add(eo_sb[:], bb_sb[:], bank(6, 2)))
                    tk_eo = S_.mark("v")
                    ffn["hev"][0] = tk_eo
                    ffn["hev"][1] = tk_eo
                    ffn["ybank"] = tk_eo
                    S_.wait("v", tk_eo)
                    S_.wait("v", tk_xo)
                    S_.v(lambda: nc.vector.tensor_add(eo_sb[:], eo_sb[:], x_tm[:]))
                    t_ = S_.mark("v")
                    S_.wait("v", t_)
                    S_.wait("v", tk_ssl)
                    S_.v(lambda: nc.vector.scalar_tensor_tensor(
                        eo_sb[:], ss_ld[:], 0.2, eo_sb[:], ALU.mult, ALU.add))
                    tk_eo = S_.mark("v")
                    op_state["edone"] = tk_eo
                    S_.wait("y", tk_eo)
                    S_.y(lambda g_=gtq: nc.sync.dma_start(
                        out_d[g_ * 128:(g_ + 1) * 128, :], eo_sb[:]))
                    op_state["out"][gtq % 2] = S_.mark(
                        "y_out0" if gtq % 2 == 0 else "y_out1")

                ffn_units = []
                for fb in range(NOWN):
                    for fc in range(32):
                        ffn_units.append(lambda fb_=fb, fc_=fc: h_full(fb_, fc_))
                    for tq in range(4):
                        for g in range(16):
                            ffn_units.append(lambda fb_=fb, tq_=tq, g_=g:
                                             y_full(fb_, tq_, g_))
                ffn_i = [0]
                n_steps = 8 * (T // 512) * KC
                stride = max(n_steps // (len(ffn_units) + 1), 1)
                step_no = [0]

                def pump():
                    step_no[0] += 1
                    while ffn_i[0] < len(ffn_units) and \
                            ffn_i[0] * stride < step_no[0]:
                        ffn_units[ffn_i[0]]()
                        ffn_i[0] += 1

                NQB = T // 512
                NKCG = KC // 16
                assert NKCG * 16 == KC
                flat = [(hp, qb, kcg) for qb in range(NQB) for hp in range(8)
                        for kcg in range(NKCG)]
                tk_k = {}
                tk_v = {}
                tk_q = [None, None]
                tk_k_use = {}
                tk_v_use = {}
                tk_q_use = [None, None]
                tk_oev = None
                tk_last_sE = None
                tk_last_sE2 = None
                tk_oevac = None
                tk_rbcc = [None, None]

                def load_kv(fi):
                    if fi >= len(flat):
                        return
                    hp2, qb2, kcg2 = flat[fi]
                    par = fi % 3
                    S_.wait("y", tk_k_use.get(fi - 3))
                    if fi <= 2:
                        S_.wait("y", tk_bias)
                        S_.wait("y", tk_kst_fin[0])
                        S_.wait("y", tk_kst_fin[1])
                    ksrc = kscr[hp2 * 128:(hp2 + 1) * 128,
                                kcg2 * 2048:(kcg2 + 1) * 2048]
                    S_.y(lambda d=kch_sb[:, par * 2048:(par + 1) * 2048], s=ksrc:
                         nc.sync.dma_start(d, s))
                    tk_k[fi] = S_.mark(("y_k0", "y_k1", "y_k2")[par])
                    S_.wait("g", tk_v_use.get(fi - 3))
                    if fi <= 2:
                        S_.wait("g", tk_bias)
                        S_.wait("g", tk_vst_fin[0])
                        S_.wait("g", tk_vst_fin[1])
                        vb0 = v65_sb[:, par * 2176:(par + 1) * 2176]
                        S_.g(lambda d=vb0: nc.gpsimd.memset(d, 1.0))
                        tk_vms = S_.mark("gc")
                        S_.wait("g", tk_vms)
                    vb = v65_sb[:, par * 2176:(par + 1) * 2176]
                    for hh in range(2):
                        vsrc = vscr.rearrange("(kc p) (hh d) -> p kc hh d",
                                              p=128, d=64)[
                            :, kcg2 * 16:(kcg2 + 1) * 16, 2 * hp2 + hh, :]
                        vdst = vb.rearrange("p (kc hh c) -> p kc hh c",
                                            kc=16, hh=2)[:, :, hh, 0:64]
                        S_.g(lambda s=vsrc, d=vdst: nc.gpsimd.dma_start(d, s))
                        tk_v[fi] = S_.mark(("gd_v0", "gd_v1", "gd_v2")[par])

                def load_q(hp2, qb2):
                    par = (qb2 * 8 + hp2) % 2
                    S_.wait("y", tk_q_use[par])
                    if hp2 == 0 and qb2 == 0:
                        S_.wait("y", tk_qst_fin[0])
                        S_.wait("y", tk_qst_fin[1])
                    qsrc = qscr[hp2 * 128:(hp2 + 1) * 128,
                                qb2 * 512:(qb2 + 1) * 512]
                    S_.y(lambda d=qv_sb[:, par * 512:(par + 1) * 512], s=qsrc:
                         nc.sync.dma_start(d, s))
                    tk_q[par] = S_.mark("y_q0" if par == 0 else "y_q1")

                load_q(0, 0)
                load_kv(0)
                load_kv(1)
                load_kv(2)
                for qb in range(NQB):
                    for hp in range(8):
                        it = qb * 8 + hp
                        pq = it % 2
                        fbase = it * NKCG
                        if it + 1 < 8 * NQB:
                            load_q(*flat[(it + 1) * NKCG][0:2])
                        qv = qv_sb[:, pq * 512:(pq + 1) * 512]
                        tS = {}
                        sE = {}
                        tPV = {}

                        def emit_S(kc):
                            par = kc % 2
                            kpar = (fbase + kc // 16) % 3
                            ksl = kch_sb[:, kpar * 2048 + (kc % 16) * 128:
                                         kpar * 2048 + (kc % 16 + 1) * 128]
                            S_.t(lambda l=ksl[0:64, :], r=qv[0:64, :], p_=par:
                                 MM(bank(2 * p_), l, r, start=True, stop=True))
                            S_.t(lambda l=ksl[64:128, :], r=qv[64:128, :], p_=par:
                                 MM(bank(2 * p_ + 1), l, r, start=True, stop=True))
                            tS[kc] = S_.mark("t")

                        S_.wait("t", tk_q[pq])
                        S_.wait("t", tk_k[fbase])
                        S_.wait("t", tk_last_sE2)
                        emit_S(0)
                        S_.wait("t", tk_rbcc[0])
                        S_.wait("t", tk_rbcc[1])
                        S_.wait("t", tk_last_sE)
                        if KC > 16:
                            S_.wait("t", tk_k[fbase + 1])
                        emit_S(1)
                        for kc in range(KC):
                            par = kc % 2
                            fi = fbase + kc // 16
                            S_.wait("s", tS[kc])
                            S_.wait("s", tPV.get(kc - 2))
                            S_.s(lambda o=p_sb[:, par * 1024:(par + 1) * 1024],
                                 p_=par: nc.scalar.activation(
                                     o, ps[:, p_ * 1024:(p_ + 1) * 1024], AF.Exp,
                                     bias=bias_sb[:]))
                            sE[kc] = S_.mark("s")
                            kpar = fi % 3
                            vb = v65_sb[:, kpar * 2176:kpar * 2176 + 2176]
                            vs0 = vb[:, (kc % 16) * 136:(kc % 16) * 136 + 65]
                            vs1 = vb[:, (kc % 16) * 136 + 68:(kc % 16) * 136 + 133]
                            sta = (kc == 0)
                            spa = (kc == KC - 1)
                            S_.wait("t", sE[kc])
                            if kc == 0:
                                S_.wait("t", tk_oevac)
                            if kc % 16 == 0:
                                S_.wait("t", tk_v[fi])
                            S_.t(lambda l=vs0, r=p_sb[:, par * 1024:par * 1024 + 512],
                                 a=sta, b=spa:
                                 MM(ps[0:65, 2048:2560], l, r, start=a, stop=b))
                            S_.t(lambda l=vs1,
                                 r=p_sb[:, par * 1024 + 512:par * 1024 + 1024],
                                 a=sta, b=spa:
                                 MM(ps[0:65, 2560:3072], l, r, start=a, stop=b))
                            if kc + 2 < KC:
                                if (kc + 2) % 16 == 0:
                                    S_.wait("t", tk_k[fbase + (kc + 2) // 16])
                                emit_S(kc + 2)
                            tPV[kc] = S_.mark("t")
                            if kc % 16 == 15:
                                tk_k_use[fi] = tS[kc]
                                tk_v_use[fi] = tPV[kc]
                                load_kv(fi + 3)
                            pump()
                        tk_q_use[pq] = tS[KC - 1]
                        tk_last_sE = sE[KC - 1]
                        tk_last_sE2 = sE[KC - 2]
                        odst = o_sb[:, hp * T + qb * 512: hp * T + (qb + 1) * 512]
                        S_.wait("v", tPV[KC - 1])
                        S_.v(lambda d=odst[0:64, :]: nc.vector.tensor_copy(
                            d, ps[0:64, 4 * 512:5 * 512]))
                        S_.v(lambda d=odst[64:128, :]: nc.vector.tensor_copy(
                            d, ps[0:64, 5 * 512:6 * 512]))
                        tk_oevac = S_.mark("v")
                        tk_bcast_prev = None
                        tk_expj_prev = None
                        for j, head in ((0, 0), (1, 1)):
                            bnk = 4 + j
                            S_.wait("s", tPV[KC - 1])
                            S_.wait("s", tk_expj_prev)
                            S_.s(lambda b_=bnk: nc.scalar.activation(
                                rcl2_sb[:], ps[64:65, b_ * 512:(b_ + 1) * 512],
                                AF.Ln))
                            t_ = S_.mark("s")
                            S_.wait("s", t_)
                            S_.wait("s", tk_bcast_prev)
                            S_.s(lambda: nc.scalar.activation(
                                rcl_sb[:], rcl2_sb[:], AF.Exp, scale=-1.0))
                            t_ = S_.mark("s")
                            tk_expj_prev = t_
                            S_.wait("t", t_)
                            S_.wait("t", tk_rbcc[j])
                            if j == 0:
                                S_.wait("t", sE[KC - 1])
                            S_.t(lambda bb_=j: MM(
                                ps[0:128, (2 + bb_) * 512:(2 + bb_) * 512 + 512],
                                ones_f[0:1, 0:128], rcl_sb[:], start=True, stop=True))
                            t_ = S_.mark("t")
                            tk_bcast_prev = t_
                            S_.wait("v", t_)
                            S_.wait("v", tk_oev)
                            S_.v(lambda bb_=j: nc.vector.tensor_copy(
                                rbc_sb[:], ps[0:128, (2 + bb_) * 512:(2 + bb_) * 512 + 512]))
                            t_ = S_.mark("v")
                            tk_rbcc[j] = t_
                            S_.wait("v", t_)
                            dsl = odst[0:64, :] if head == 0 else odst[64:128, :]
                            rsl = rbc_sb[0:64, :] if head == 0 else rbc_sb[64:128, :]
                            S_.v(lambda d=dsl, r_=rsl: nc.vector.tensor_mul(
                                d, d, r_))
                            tk_oev = S_.mark("v")
                        if hp == 7:
                            op_state["oev"] = tk_oev
                            for tq2_ in range(4):
                                ffn_units.append(
                                    lambda g_=qb * 4 + tq2_: outproj_unit(g_))
                while ffn_i[0] < len(ffn_units):
                    ffn_units[ffn_i[0]]()
                    ffn_i[0] += 1
                tk_ffn_end_v = ffn["spk_end"]
                tk_ffn_end_t = ("t", S_.cnt["t"])
                tk_p3_end_s = tk_last_sE
                tk_p3_end_v = tk_oev

        for _sem in ("y_const", "gd_mask", "gd_wqkv", "gd_wout", "gd_wfc0",
                     "gd_wfc1", "gd_x0", "gd_x1", "gd_x2", "y_kst0", "y_kst1", "y_vst0",
                     "y_vst1", "y_kld0", "y_kld1", "gd_vld0", "gd_vld1",
                     "gd_wpj0", "gd_wpj1", "y_xtm", "y_out", "gd_wcast",
                     "y_wfc0", "y_wfc1", "y_qst0", "y_qst1", "y_k0", "y_k1",
                     "gd_v0", "gd_v1", "y_q0", "y_q1", "y_ss", "y_ss0",
                     "y_ss1", "y_out0", "y_out1", "y_k2", "gd_v2"):
            if _sem in S_.cnt:
                S_.wait("y", (_sem, S_.cnt[_sem]))

        def make_body(stream):
            def body(eng):
                last = None
                for kind, sem, val in S_.ops[stream]:
                    if kind == "op":
                        r = val()
                        if r is not None:
                            last = r
                    elif kind == "inc":
                        last.then_inc(sems[sem], val)
                    else:
                        eng.wait_ge(sems[sem], val)
            return body

        with nc.Block() as block:
            block.tensor(make_body("t"))
            block.vector(make_body("v"))
            block.scalar(make_body("s"))
            block.gpsimd(make_body("g"))
            block.sync(make_body("y"))

    return nc


def _consts():
    f = np.arange(NF)
    n = np.arange(DH)
    w = np.where((f == 0) | (f == NF - 1), 1.0, 2.0) / (DH * 8.0)
    cosb = np.cos(2 * np.pi * np.outer(f, n) / DH)
    sinb = np.sin(2 * np.pi * np.outer(f, n) / DH)
    cc = np.zeros((128, 128), np.float32)
    cs = np.zeros((128, 128), np.float32)
    cc[0:NF, 0:DH] = cosb
    cc[64:64 + NF, 64:64 + DH] = cosb
    cs[0:NF, 0:DH] = sinb
    cs[64:64 + NF, 64:64 + DH] = sinb
    wv = np.zeros((128, 1), np.float32)
    wv[0:NF, 0] = w
    wv[64:64 + NF, 0] = w
    return cc, cs, wv


def make_in_maps(x, wqkv, wout, bout, mask, wfc, wproj, n_cores=8):
    x = np.asarray(x, np.float32)
    B, S, _ = x.shape
    T = (B * S) // n_cores
    cc, cs, wv = _consts()
    wqkv = np.ascontiguousarray(np.asarray(wqkv, np.float32))
    wout = np.ascontiguousarray(np.asarray(wout, np.float32))
    bout = np.asarray(bout, np.float32).reshape(1, D)
    mask = np.asarray(mask, np.float32).reshape(NF, 1)
    wfcT = np.ascontiguousarray(np.asarray(wfc, np.float32).T)
    wprojT = np.ascontiguousarray(np.asarray(wproj, np.float32).T)
    per_seq = n_cores // B
    in_maps = []
    for c in range(n_cores):
        b, r = divmod(c, per_seq)
        xr = np.roll(x[b], -T * r, axis=0)
        in_maps.append({
            "xT": np.ascontiguousarray(xr.T),
            "x_own": np.ascontiguousarray(xr[0:T]),
            "wqkv": wqkv, "wout": wout, "bout": bout,
            "wfcT": wfcT, "wprojT": wprojT,
            "mask_l": mask, "ccos": cc, "csin": cs, "wvec": wv,
        })
    return in_maps, B, S, T


def kernel(L0_attn_x, L0_attn_w_qkv, L0_attn_w_out, L0_attn_b_out,
           L0_attn_mask_logits, L0_ffn_W_fc, L0_ffn_W_proj):
    in_maps, B, S, T = make_in_maps(
        L0_attn_x, L0_attn_w_qkv, L0_attn_w_out, L0_attn_b_out,
        L0_attn_mask_logits, L0_ffn_W_fc, L0_ffn_W_proj)
    nc = build({"S": S, "T": T})
    res = run_bass_kernel_spmd(nc, in_maps, core_ids=list(range(8)))
    per_seq = 8 // B
    out = np.empty((B, S, D), np.float32)
    for c in range(8):
        b, r = divmod(c, per_seq)
        out[b, r * T:(r + 1) * T] = np.asarray(res.results[c]["out"], np.float32)
    return out


# version 19 (final)
# speedup vs baseline: 1.2280x; 1.0067x over previous
import numpy as np
from contextlib import ExitStack

import concourse.bass as bass
import concourse.mybir as mybir
from concourse.bass_utils import run_bass_kernel_spmd

F32 = mybir.dt.float32
BF16 = mybir.dt.bfloat16
AF = mybir.ActivationFunctionType
ALU = mybir.AluOpType
AX = mybir.AxisListType

D = 1024
NF = 33
DH = 64
FF = 4096


class Sched:

    STREAMS = ("t", "v", "s", "g", "y")

    def __init__(self):
        self.ops = {k: [] for k in self.STREAMS}
        self.cnt = {}
        self._lastw = {}

    def _emit(self, stream, fn):
        self.ops[stream].append(("op", None, fn))

    def t(self, fn):
        self._emit("t", fn)

    def v(self, fn):
        self._emit("v", fn)

    def s(self, fn):
        self._emit("s", fn)

    def g(self, fn):
        self._emit("g", fn)

    def y(self, fn):
        self._emit("y", fn)

    def mark(self, sem):
        if sem.startswith("gd") or sem == "gc":
            stream = "g"
        elif sem.startswith("y"):
            stream = "y"
        else:
            stream = sem
        inc = 16 if (sem.startswith("gd") or sem.startswith("y")) else 1
        for kind_, sem_, _v in reversed(self.ops[stream]):
            if kind_ == "op":
                break
            if kind_ == "inc":
                if sem_ == sem:
                    return (sem, self.cnt[sem])
                raise RuntimeError(f"two sems on one op: {sem_} vs {sem}")
        self.cnt[sem] = self.cnt.get(sem, 0) + inc
        self.ops[stream].append(("inc", sem, inc))
        return (sem, self.cnt[sem])

    def wait(self, stream, tok):
        if tok is None:
            return
        sem, val = tok
        key = (stream, sem)
        if self._lastw.get(key, -1) >= val:
            return
        self._lastw[key] = val
        self.ops[stream].append(("wait", sem, val))


def build(cfg):
    S = cfg["S"]
    T = cfg["T"]
    NBLK = S // 512
    NOWN = T // 512
    KC = S // 128
    NQP = T // 1024
    assert NOWN * 512 == T and NQP * 1024 == T and NBLK * 512 == S

    nc = bass.Bass("TRN2", target_bir_lowering=False)

    xT = nc.dram_tensor("xT", [D, S], F32, kind="ExternalInput").ap()
    x_own = nc.dram_tensor("x_own", [T, D], F32, kind="ExternalInput").ap()
    wqkv = nc.dram_tensor("wqkv", [D, 3 * D], F32, kind="ExternalInput").ap()
    wout = nc.dram_tensor("wout", [D, D], F32, kind="ExternalInput").ap()
    bout = nc.dram_tensor("bout", [1, D], F32, kind="ExternalInput").ap()
    wfcT = nc.dram_tensor("wfcT", [D, FF], F32, kind="ExternalInput").ap()
    wprojT = nc.dram_tensor("wprojT", [FF, D], F32, kind="ExternalInput").ap()
    mask_l = nc.dram_tensor("mask_l", [NF, 1], F32, kind="ExternalInput").ap()
    ccos = nc.dram_tensor("ccos", [128, 128], F32, kind="ExternalInput").ap()
    csin = nc.dram_tensor("csin", [128, 128], F32, kind="ExternalInput").ap()
    wvec = nc.dram_tensor("wvec", [128, 1], F32, kind="ExternalInput").ap()
    out_d = nc.dram_tensor("out", [T, D], F32, kind="ExternalOutput").ap()
    kscr = nc.dram_tensor("kscr", [D, S], BF16, kind="Internal").ap()
    vscr = nc.dram_tensor("vscr", [S, D], BF16, kind="Internal").ap()
    wfcb = nc.dram_tensor("wfcb", [D, FF], BF16, kind="Internal").ap()
    qscr = nc.dram_tensor("qscr", [D, T], BF16, kind="Internal").ap()
    sscr = nc.dram_tensor("sscr", [T, D], BF16, kind="Internal").ap()
    wpjb = nc.dram_tensor("wpjb", [FF, D], BF16, kind="Internal").ap()

    S_ = Sched()
    VW = KC * 2 * 68

    with ExitStack() as ctx:
        e = ctx.enter_context
        xn_own = e(nc.sbuf_tensor("xn_own", [128, 8 * T], BF16))
        o_sb = e(nc.sbuf_tensor("o_sb", [128, 8 * T], BF16))
        bb_sb = e(nc.sbuf_tensor("bb_sb", [128, D], F32))
        bias_sb = e(nc.sbuf_tensor("bias_sb", [128, 1], F32))
        g_sb = e(nc.sbuf_tensor("g_sb", [128, 128], BF16))
        ones_bf = e(nc.sbuf_tensor("ones_bf", [128, 1], BF16))
        ones_f = e(nc.sbuf_tensor("ones_f", [1, 128], F32))
        ps = e(nc.psum_tensor("ps", [128, 4096], F32))

        def bank(b, n=1):
            return ps[:, b * 512:(b + n) * 512]

        from concourse.hw_specs import get_activation_tables

        def preload_set(name):
            idx = list(get_activation_tables(nc.m.arch).keys()).index(name)
            return nc.scalar.add_instruction(mybir.InstLoadActFuncSet(
                name=nc.get_next_instruction_name(), act_func_set_id=idx,
                ins=[], outs=[]))

        sems = {k: e(nc.semaphore("sem_" + k)) for k in (
            "t", "v", "s", "gc", "y_const", "gd_mask", "gd_wqkv", "gd_wout",
            "gd_wfc0", "gd_wfc1", "gd_x0", "gd_x1", "gd_x2", "y_kst0", "y_kst1",
            "y_vst0", "y_vst1", "y_kld0", "y_kld1", "gd_vld0", "gd_vld1",
            "gd_wpj0", "gd_wpj1", "y_xtm", "y_out", "gd_wcast",
            "y_wfc0", "y_wfc1", "y_qst0", "y_qst1", "y_k0", "y_k1",
            "gd_v0", "gd_v1", "y_q0", "y_q1", "y_ss", "y_ss0", "y_ss1",
            "y_out0", "y_out1", "y_k2", "gd_v2")}
        MM = nc.tensor.matmul

        with ExitStack() as qp_ctx:
            qst_sb = qp_ctx.enter_context(nc.sbuf_tensor("qst_sb", [128, 2 * 512], BF16))

            with ExitStack() as c1:
                e1 = c1.enter_context
                wqkv_sb = e1(nc.sbuf_tensor("wqkv_sb", [128, 8 * 3 * D], BF16))
                brow = e1(nc.sbuf_tensor("brow", [1, D], F32))
                x_sb = e1(nc.sbuf_tensor("x_sb", [128, 3 * 4096], BF16))
                xsq_sb = e1(nc.sbuf_tensor("xsq_sb", [128, 2 * 512], BF16))
                xnt_sb = e1(nc.sbuf_tensor("xnt_sb", [128, 1024], BF16))
                kst_sb = e1(nc.sbuf_tensor("kst_sb", [128, 2 * 512], BF16))
                vst_sb = e1(nc.sbuf_tensor("vst_sb", [128, 2 * 1024], BF16))
                qt_sb = e1(nc.sbuf_tensor("qt_sb", [128, 512], BF16))
                sq_sb = e1(nc.sbuf_tensor("sq_sb", [128, 2 * 512], BF16))
                mb_sb = e1(nc.sbuf_tensor("mb_sb", [128, 1024], BF16))
                rb_sb = e1(nc.sbuf_tensor("rb_sb", [128, 1024], F32))
                mrow = e1(nc.sbuf_tensor("mrow", [1, 512], F32))
                r1 = e1(nc.sbuf_tensor("r1", [1, 512], F32))
                r2 = e1(nc.sbuf_tensor("r2", [1, 512], F32))
                r3 = e1(nc.sbuf_tensor("r3", [1, 512], F32))
                s2_sb = e1(nc.sbuf_tensor("s2_sb", [128, 1], F32))
                cs_sb = e1(nc.sbuf_tensor("cs_sb", [128, 128], F32))
                ss_sb = e1(nc.sbuf_tensor("ss_sb", [128, 128], F32))
                ccos_sb = e1(nc.sbuf_tensor("ccos_sb", [128, 128], F32))
                csin_sb = e1(nc.sbuf_tensor("csin_sb", [128, 128], F32))
                wvec_sb = e1(nc.sbuf_tensor("wvec_sb", [128, 1], F32))
                mk2_sb = e1(nc.sbuf_tensor("mk2_sb", [1, 1], F32))
                mq2_sb = e1(nc.sbuf_tensor("mq2_sb", [1, 1], F32))
                t11 = e1(nc.sbuf_tensor("t11", [1, 8], F32))
                eps_sb = e1(nc.sbuf_tensor("eps_sb", [1, 1], F32))

                S_.y(lambda: nc.sync.dma_start(ccos_sb[:], ccos))
                S_.mark("y_const")
                S_.y(lambda: nc.sync.dma_start(csin_sb[:], csin))
                S_.mark("y_const")
                S_.y(lambda: nc.sync.dma_start(wvec_sb[:], wvec))
                S_.mark("y_const")
                S_.y(lambda: nc.sync.dma_start(brow[:], bout))
                tk_consts = S_.mark("y_const")

                S_.v(lambda: nc.vector.memset(s2_sb[:], 0.0))
                tk_ms = S_.mark("v")
                S_.wait("g", tk_ms)
                S_.g(lambda: nc.gpsimd.dma_start(s2_sb[0:NF, :], mask_l))
                S_.mark("gd_mask")
                S_.g(lambda: nc.gpsimd.dma_start(s2_sb[64:64 + NF, :], mask_l))
                tk_mask = S_.mark("gd_mask")

                S_.v(lambda: nc.vector.memset(ones_bf[:], 1.0))
                S_.v(lambda: nc.vector.memset(ones_f[:], 1.0))
                S_.v(lambda: nc.vector.memset(mk2_sb[:], 0.0))
                S_.v(lambda: nc.vector.memset(mq2_sb[:], 0.0))
                S_.v(lambda: nc.vector.memset(eps_sb[:], 1e-5))
                tk_ones = S_.mark("v")

                S_.s(lambda: nc.scalar.preload_act_table(AF.Sigmoid))
                S_.wait("s", tk_mask)
                S_.s(lambda: nc.scalar.activation(s2_sb[:], s2_sb[:], AF.Sigmoid))
                tk_sg = S_.mark("s")
                S_.wait("v", tk_sg)
                S_.v(lambda: nc.vector.tensor_mul(s2_sb[:], s2_sb[:], s2_sb[:]))
                t_ = S_.mark("v")
                S_.wait("v", t_)
                S_.wait("v", tk_consts)
                S_.v(lambda: nc.vector.tensor_scalar_mul(s2_sb[:], s2_sb[:], wvec_sb[:]))
                t_ = S_.mark("v")
                S_.wait("v", t_)
                S_.v(lambda: nc.vector.tensor_scalar_mul(cs_sb[:], ccos_sb[:], s2_sb[:]))
                S_.v(lambda: nc.vector.tensor_scalar_mul(ss_sb[:], csin_sb[:], s2_sb[:]))
                tk_cs = S_.mark("v")
                S_.wait("t", tk_cs)
                S_.t(lambda: MM(ps[0:128, 0:128], cs_sb[:], ccos_sb[:],
                                start=True, stop=False))
                S_.t(lambda: MM(ps[0:128, 0:128], ss_sb[:], csin_sb[:],
                                start=False, stop=True))
                tk_gmm = S_.mark("t")
                S_.wait("v", tk_gmm)
                S_.v(lambda: nc.vector.tensor_copy(g_sb[:], ps[0:128, 0:128]))
                tk_g = S_.mark("v")
                S_.wait("t", tk_g)
                S_.t(lambda: MM(bank(0), ones_f[:], brow[0:1, 0:512],
                                start=True, stop=True))
                S_.t(lambda: MM(bank(1), ones_f[:], brow[0:1, 512:1024],
                                start=True, stop=True))
                tk_bbm = S_.mark("t")
                S_.wait("v", tk_bbm)
                S_.v(lambda: nc.vector.tensor_copy(bb_sb[:, 0:512], bank(0)))
                S_.v(lambda: nc.vector.tensor_copy(bb_sb[:, 512:1024], bank(1)))
                tk_bb = S_.mark("v")

                S_.s(lambda: preload_set("natural_log_exp_and_others"))

                tok_xn = [None] * NBLK
                tk_kst_dma = [None, None]
                tk_qst_dma = [None, None]
                tk_vst_dma = [None, None]
                prev_m1 = None
                prev_lnmm = None
                tk_mbc_last = None
                tk_kn2v = None
                tk_qn2v = None
                prev_kev = None
                prev_vev = None
                prev_ksq_mm = None
                prev_qg = None
                prev_qpev = None
                tk_blk_t = [None] * NBLK

                xv_save = {}

                def ln_stats(blk):
                    nonlocal prev_m1
                    par = blk % 3
                    xv = x_sb[:, par * 4096:par * 4096 + 4096]
                    xv_save[blk] = xv
                    srcx = xT.rearrange("(dc p) s -> p dc s", p=128)[
                        :, :, blk * 512:(blk + 1) * 512]
                    dstx = xv.rearrange("p (dc s) -> p dc s", dc=8)
                    if blk >= 3:
                        S_.wait("g", tok_xn[blk - 3])
                        S_.wait("g", tk_blk_t[blk - 3])
                    S_.g(lambda s=srcx, d=dstx: nc.gpsimd.dma_start(d, s))
                    tk_x = S_.mark(("gd_x0", "gd_x1", "gd_x2")[par])

                    S_.wait("t", tk_x)
                    S_.wait("t", tk_ones)
                    if blk == 0:
                        S_.wait("t", tk_bb)
                    if prev_m1 is not None:
                        S_.wait("t", prev_m1)
                    if tk_mbc_last is not None:
                        S_.wait("t", tk_mbc_last)
                    for dc in range(8):
                        S_.t(lambda r=xv[:, dc * 512:(dc + 1) * 512],
                             st=(dc == 0), sp=(dc == 7):
                             MM(bank(0)[0:1, :], ones_bf[:], r, start=st, stop=sp))
                    tk_sx = S_.mark("t")
                    tk_xm = {}
                    tk_x2mm = {}
                    for dc in range(8):
                        xq = xsq_sb[:, (dc % 2) * 512:(dc % 2) * 512 + 512]
                        if dc >= 2:
                            S_.wait("v", tk_x2mm[dc - 2])
                        S_.wait("v", tk_x)
                        S_.v(lambda a=xq, b=xv[:, dc * 512:(dc + 1) * 512]:
                             nc.vector.tensor_mul(a, b, b))
                        tk_xm[dc] = S_.mark("v")
                        if dc >= 1:
                            S_.wait("t", tk_xm[dc - 1])
                            S_.t(lambda r=xsq_sb[:, ((dc - 1) % 2) * 512:
                                              ((dc - 1) % 2) * 512 + 512],
                                 st=(dc - 1 == 0), sp=False:
                                 MM(bank(1)[0:1, :], ones_bf[:], r, start=st, stop=sp))
                            tk_x2mm[dc - 1] = S_.mark("t")
                    S_.wait("t", tk_xm[7])
                    S_.t(lambda r=xsq_sb[:, (7 % 2) * 512:(7 % 2) * 512 + 512]:
                         MM(bank(1)[0:1, :], ones_bf[:], r, start=False, stop=True))
                    tk_x2mm[7] = S_.mark("t")
                    if prev_lnmm is not None:
                        S_.wait("v", prev_lnmm)
                    S_.wait("v", tk_sx)
                    S_.wait("v", tk_x2mm[7])
                    S_.v(lambda: nc.vector.tensor_scalar_mul(mrow[:], bank(0)[0:1, :], 1.0 / D))
                    S_.v(lambda: nc.vector.tensor_scalar_mul(r1[:], bank(1)[0:1, :], 1.0 / D))
                    prev_m1 = S_.mark("v")
                    S_.wait("v", prev_m1)
                    S_.v(lambda: nc.vector.tensor_mul(r2[:], mrow[:], mrow[:]))
                    t_ = S_.mark("v")
                    S_.wait("v", t_)
                    S_.v(lambda: nc.vector.tensor_sub(r1[:], r1[:], r2[:]))
                    tk_var = S_.mark("v")
                    S_.wait("s", tk_var)
                    S_.wait("s", tk_ones)
                    S_.s(lambda: nc.scalar.activation(r2[:], r1[:], AF.Ln, bias=eps_sb[:]))
                    t_ = S_.mark("s")
                    S_.wait("s", t_)
                    S_.s(lambda: nc.scalar.activation(r3[:], r2[:], AF.Exp, scale=-0.5))
                    tk_rstd_a[blk] = S_.mark("s")

                pending_xn = []

                def ln_apply(blk):
                    nonlocal prev_lnmm, tk_mbc_last
                    xv = xv_save[blk]
                    S_.wait("t", tk_rstd_a[blk])
                    if prev_m1 is not None:
                        S_.wait("t", prev_m1)
                    S_.t(lambda: MM(bank(0), ones_f[:], mrow[:], start=True, stop=True))
                    S_.t(lambda: MM(bank(1), ones_f[:], r3[:], start=True, stop=True))
                    prev_lnmm = S_.mark("t")
                    S_.wait("s", prev_lnmm)
                    if blk > 0:
                        S_.wait("s", tok_xn[blk - 1])
                    S_.s(lambda: nc.scalar.activation(mb_sb[:, 0:512], bank(0), AF.Copy))
                    S_.s(lambda: nc.scalar.activation(mb_sb[:, 512:1024], bank(0), AF.Copy))
                    S_.s(lambda: nc.scalar.activation(rb_sb[:, 0:512], bank(1), AF.Copy))
                    S_.s(lambda: nc.scalar.activation(rb_sb[:, 512:1024], bank(1), AF.Copy))
                    tk_mbc = S_.mark("s")
                    tk_mbc_last = tk_mbc
                    if blk < NOWN:
                        xn_dst = xn_own[:, blk * 4096:(blk + 1) * 4096]
                    else:
                        xn_dst = xv
                    xn_save[blk] = xn_dst
                    pending_xn.append(lambda: ln_apply_xn(blk, xv, xn_dst, tk_mbc))

                def ln_apply_xn(blk, xv, xn_dst, tk_mbc):
                    S_.wait("v", tk_mbc)
                    for dc2 in range(4):
                        xc = xv[:, dc2 * 1024:(dc2 + 1) * 1024]
                        xo = xn_dst[:, dc2 * 1024:(dc2 + 1) * 1024]
                        S_.v(lambda a=xnt_sb[:], b=xc:
                             nc.vector.tensor_sub(a, b, mb_sb[:]))
                        t_ = S_.mark("v")
                        S_.wait("v", t_)
                        S_.v(lambda a=xo, b=xnt_sb[:]:
                             nc.vector.tensor_mul(a, b, rb_sb[:]))
                        t_ = S_.mark("v")
                        S_.wait("v", t_)
                    tok_xn[blk] = t_

                def stage_kvq(blk):
                    nonlocal tk_kn2v, tk_qn2v, prev_kev, prev_vev, prev_ksq_mm, prev_qg, prev_qpev
                    par = blk % 2
                    xn_src = xn_save[blk]
                    tk_ksq = {}
                    S_.wait("t", tok_xn[blk])
                    S_.wait("t", tk_wqkv)
                    if tk_kn2v is not None:
                        S_.wait("t", tk_kn2v)
                    if tk_qn2v is not None:
                        S_.wait("t", tk_qn2v)
                    if prev_qpev is not None:
                        S_.wait("t", prev_qpev)
                    kev_par = getattr(S_, "_kev_par", [None, None, None, None])
                    S_._kev_par = kev_par
                    for dc in range(8):
                        kbk = 4 + dc % 4
                        kdst = kst_sb[:, (dc % 2) * 512:(dc % 2) * 512 + 512]
                        if kev_par[dc % 4] is not None:
                            S_.wait("t", kev_par[dc % 4])
                        for di in range(8):
                            lhs = wqkv_sb[:, di * 3 * D + D + dc * 128:
                                          di * 3 * D + D + (dc + 1) * 128]
                            rhs = xn_src[:, di * 512:(di + 1) * 512]
                            S_.t(lambda l=lhs, r=rhs, b=kbk, st=(di == 0), sp=(di == 7):
                                 MM(bank(b), l, r, start=st, stop=sp))
                        tk_kmm = S_.mark("t")
                        S_.wait("s", tk_kmm)
                        S_.wait("s", tk_kst_dma[dc % 2])
                        S_.s(lambda d=kdst, b=kbk: nc.scalar.activation(d, bank(b), AF.Copy))
                        prev_kev = S_.mark("s")
                        kev_par[dc % 4] = prev_kev
                        if prev_ksq_mm is not None:
                            S_.wait("v", prev_ksq_mm)
                        S_.wait("v", prev_kev)
                        S_.v(lambda d=kdst, q=sq_sb[:, (dc % 2) * 512:(dc % 2) * 512 + 512]:
                             nc.vector.tensor_mul(q, d, d))
                        tk_ksq[dc] = S_.mark("v")
                        if dc >= 1:
                            S_.wait("t", tk_ksq[dc - 1])
                            S_.t(lambda q=sq_sb[:, ((dc - 1) % 2) * 512:
                                              ((dc - 1) % 2) * 512 + 512],
                                 st=(dc - 1 == 0):
                                 MM(bank(2)[0:1, :], ones_bf[:], q, start=st, stop=False))
                            prev_ksq_mm = S_.mark("t")
                        S_.wait("y", prev_kev)
                        kdd = kscr.rearrange("(dc p) s -> dc p s", p=128)[dc][
                            :, blk * 512:(blk + 1) * 512]
                        S_.y(lambda dd=kdd, sd=kdst: nc.sync.dma_start(dd, sd))
                        tk_kst_dma[dc % 2] = S_.mark("y_kst0" if dc % 2 == 0 else "y_kst1")
                    S_.wait("t", tk_ksq[7])
                    S_.t(lambda q=sq_sb[:, (7 % 2) * 512:(7 % 2) * 512 + 512]:
                         MM(bank(2)[0:1, :], ones_bf[:], q, start=False, stop=True))
                    prev_ksq_mm = S_.mark("t")
                    if tk_kn2v is not None:
                        S_.wait("v", tk_kn2v)
                    S_.wait("v", prev_ksq_mm)
                    S_.v(lambda: nc.vector.reduce_max(t11[0:1, 0:1], bank(2)[0:1, :], axis=AX.X))
                    t_ = S_.mark("v")
                    S_.wait("v", t_)
                    S_.v(lambda: nc.vector.tensor_max(mk2_sb[:], mk2_sb[:], t11[0:1, 0:1]))
                    tk_kn2v = S_.mark("v")

                    vev_par = kev_par
                    for tq in range(4):
                        vdst = vst_sb[:, (tq % 2) * 1024:(tq % 2) * 1024 + 1024]
                        for nn in range(2):
                            gidx = (tq * 2 + nn) % 4
                            vbk = 4 + gidx
                            if vev_par[gidx] is not None:
                                S_.wait("t", vev_par[gidx])
                            for di in range(8):
                                lhs = xn_src[:, di * 512 + tq * 128: di * 512 + (tq + 1) * 128]
                                rh = wqkv_sb[:, di * 3 * D + 2 * D + nn * 512:
                                             di * 3 * D + 2 * D + (nn + 1) * 512]
                                S_.t(lambda l=lhs, r=rh, b=vbk, st=(di == 0), sp=(di == 7):
                                     MM(bank(b), l, r, start=st, stop=sp))
                            tk_vmm = S_.mark("t")
                            S_.wait("s", tk_vmm)
                            S_.wait("s", tk_vst_dma[tq % 2])
                            S_.s(lambda d=vdst, n=nn, b=vbk:
                                 nc.scalar.activation(d[:, n * 512:(n + 1) * 512],
                                                      bank(b), AF.Copy))
                            prev_vev = S_.mark("s")
                            vev_par[gidx] = prev_vev
                        S_.wait("y", prev_vev)
                        vdd = vscr[blk * 512 + tq * 128: blk * 512 + (tq + 1) * 128, :]
                        S_.y(lambda dd=vdd, sd=vdst: nc.sync.dma_start(dd, sd))
                        tk_vst_dma[tq % 2] = S_.mark("y_vst0" if tq % 2 == 0 else "y_vst1")
                        if tq == 1 and pending_xn:
                            pending_xn.pop(0)()

                    if blk < NOWN:
                        tk_qn2_mm = None
                        tk_qsq = {}
                        for hp in range(8):
                            if prev_qg is not None:
                                S_.wait("t", prev_qg)
                            S_.wait("t", kev_par[3])
                            for di in range(8):
                                lhs = wqkv_sb[:, di * 3 * D + hp * 128:
                                              di * 3 * D + (hp + 1) * 128]
                                rhs = xn_src[:, di * 512:(di + 1) * 512]
                                S_.t(lambda l=lhs, r=rhs, st=(di == 0), sp=(di == 7):
                                     MM(bank(7), l, r, start=st, stop=sp))
                            tk_qmm = S_.mark("t")
                            S_.wait("s", tk_qmm)
                            if prev_qg is not None:
                                S_.wait("s", prev_qg)
                            S_.s(lambda: nc.scalar.activation(qt_sb[:], bank(7), AF.Copy))
                            tk_qev = S_.mark("s")
                            S_.wait("t", tk_qev)
                            S_.wait("t", tk_g)
                            S_.wait("t", kev_par[0])
                            if prev_qpev is not None:
                                S_.wait("t", prev_qpev)
                            S_.t(lambda: MM(bank(4), g_sb[:], qt_sb[:],
                                            start=True, stop=True))
                            prev_qg = S_.mark("t")
                            S_.wait("s", prev_qg)
                            qdst = qst_sb[:, (hp % 2) * 512:(hp % 2) * 512 + 512]
                            S_.wait("s", tk_qst_dma[hp % 2])
                            S_.s(lambda d=qdst: nc.scalar.activation(d, bank(4), AF.Copy))
                            prev_qpev = S_.mark("s")
                            S_.wait("y", prev_qpev)
                            qdd = qscr[hp * 128:(hp + 1) * 128,
                                       blk * 512:(blk + 1) * 512]
                            S_.y(lambda dd=qdd, sd=qdst: nc.sync.dma_start(dd, sd))
                            tk_qst_dma[hp % 2] = S_.mark(
                                "y_qst0" if hp % 2 == 0 else "y_qst1")
                            if prev_ksq_mm is not None:
                                S_.wait("v", prev_ksq_mm)
                            if tk_qn2_mm is not None:
                                S_.wait("v", tk_qn2_mm)
                            S_.wait("v", prev_qpev)
                            S_.v(lambda d=qdst, q=sq_sb[:, (hp % 2) * 512:
                                              (hp % 2) * 512 + 512]:
                                 nc.vector.tensor_mul(q, d, d))
                            tk_qsq[hp] = S_.mark("v")
                            if hp >= 1:
                                S_.wait("t", tk_qsq[hp - 1])
                                S_.t(lambda q=sq_sb[:, ((hp - 1) % 2) * 512:
                                                  ((hp - 1) % 2) * 512 + 512],
                                     st=(hp - 1 == 0):
                                     MM(bank(3)[0:1, :], ones_bf[:], q,
                                        start=st, stop=False))
                                tk_qn2_mm = S_.mark("t")
                                prev_ksq_mm = tk_qn2_mm
                        S_.wait("t", tk_qsq[7])
                        S_.t(lambda q=sq_sb[:, (7 % 2) * 512:(7 % 2) * 512 + 512]:
                             MM(bank(3)[0:1, :], ones_bf[:], q, start=False, stop=True))
                        tk_qn2_mm = S_.mark("t")
                        prev_ksq_mm = tk_qn2_mm
                        S_.wait("v", tk_qn2_mm)
                        S_.v(lambda: nc.vector.reduce_max(t11[0:1, 1:2], bank(3)[0:1, :],
                                                          axis=AX.X))
                        t_ = S_.mark("v")
                        S_.wait("v", t_)
                        S_.v(lambda: nc.vector.tensor_max(mq2_sb[:], mq2_sb[:], t11[0:1, 1:2]))
                        tk_qn2v = S_.mark("v")
                    tk_blk_t[blk] = S_.mark("t")

                xn_save = {}
                tk_rstd_a = {}
                ln_stats(0)
                for dc in range(8):
                    srcw = wqkv.rearrange("(dc p) n -> dc p n", p=128)[dc]
                    dstw = wqkv_sb[:, dc * 3 * D:(dc + 1) * 3 * D]
                    S_.g(lambda s=srcw, d=dstw: nc.gpsimd.dma_start(d, s))
                    tk_wqkv = S_.mark("gd_wqkv")
                ln_apply(0)
                pending_xn.pop(0)()
                if NBLK > 1:
                    ln_stats(1)
                for ch in range(8):
                    S_.g(lambda c=ch: nc.gpsimd.dma_start(
                        wfcb[c * 128:(c + 1) * 128, :], wfcT[c * 128:(c + 1) * 128, :]))
                    tk_wcast = S_.mark("gd_wcast")
                for ch in range(8):
                    S_.g(lambda c=ch: nc.gpsimd.dma_start(
                        wpjb[c * 512:(c + 1) * 512, :], wprojT[c * 512:(c + 1) * 512, :]))
                    tk_wcast = S_.mark("gd_wcast")
                for blk in range(NBLK):
                    if blk + 1 < NBLK:
                        ln_apply(blk + 1)
                    stage_kvq(blk)
                    if blk + 2 < NBLK:
                        ln_stats(blk + 2)

                tk_qst_fin = [("y_qst0", S_.cnt.get("y_qst0", 0)),
                              ("y_qst1", S_.cnt.get("y_qst1", 0))]
                tk_kst_fin = [("y_kst0", S_.cnt.get("y_kst0", 0)),
                              ("y_kst1", S_.cnt.get("y_kst1", 0))]
                tk_vst_fin = [("y_vst0", S_.cnt.get("y_vst0", 0)),
                              ("y_vst1", S_.cnt.get("y_vst1", 0))]
                S_.wait("v", tk_kn2v)
                S_.wait("v", tk_qn2v)
                S_.v(lambda: nc.vector.tensor_mul(t11[0:1, 2:3], mq2_sb[:], mk2_sb[:]))
                t_ = S_.mark("v")
                S_.wait("s", t_)
                S_.s(lambda: nc.scalar.activation(t11[0:1, 4:5], t11[0:1, 2:3], AF.Ln))
                t_ = S_.mark("s")
                S_.wait("s", t_)
                S_.s(lambda: nc.scalar.activation(t11[0:1, 3:4], t11[0:1, 4:5], AF.Exp,
                                                  scale=0.5))
                tk_M = S_.mark("s")
                S_.wait("t", tk_M)
                S_.wait("t", tk_kn2v)
                S_.t(lambda: MM(ps[0:128, 0:1], ones_f[:], t11[0:1, 3:4],
                                start=True, stop=True))
                t_ = S_.mark("t")
                S_.wait("v", t_)
                S_.v(lambda: nc.vector.tensor_scalar_mul(bias_sb[:], ps[0:128, 0:1], -1.0))
                tk_bias = S_.mark("v")

            tk_p1_t = ("t", S_.cnt["t"])
            with ExitStack() as c3:
                e3 = c3.enter_context
                kch_sb = e3(nc.sbuf_tensor("kch_sb", [128, 3 * 2048], BF16))
                v65_sb = e3(nc.sbuf_tensor("v65_sb", [128, 3 * 16 * 136], BF16))
                qv_sb = e3(nc.sbuf_tensor("qv_sb", [128, 2 * 512], BF16))
                p_sb = e3(nc.sbuf_tensor("p_sb", [128, 2 * 1024], BF16))
                rcl_sb = e3(nc.sbuf_tensor("rcl_sb", [1, 1024], F32))
                rcl2_sb = e3(nc.sbuf_tensor("rcl2_sb", [1, 1024], F32))
                rbc_sb = e3(nc.sbuf_tensor("rbc_sb", [128, 512], F32))
                wfc_sb = e3(nc.sbuf_tensor("wfc_sb", [128, 2 * 8 * 512], BF16))
                wpj_sb = e3(nc.sbuf_tensor("wpj_sb", [128, 2 * 2 * 1024], BF16))
                wout_sb = e3(nc.sbuf_tensor("wout_sb", [128, 8 * D], BF16))
                ss_ld = e3(nc.sbuf_tensor("ss_ld", [128, 1024], BF16))
                x_tm = e3(nc.sbuf_tensor("x_tm", [128, D], F32))
                eo_sb = e3(nc.sbuf_tensor("eo_sb", [128, D], F32))
                h_sb = e3(nc.sbuf_tensor("h_sb", [128, 32 * 512], BF16))
                ssb_st = e3(nc.sbuf_tensor("ssb_st", [128, 1024], BF16))
                y_sb = e3(nc.sbuf_tensor("y_sb", [128, D], F32))
                mem_sb = e3(nc.sbuf_tensor("mem_sb", [128, D], F32))
                spk_sb = e3(nc.sbuf_tensor("spk_sb", [128, D], BF16))
                ssum_sb = e3(nc.sbuf_tensor("ssum_sb", [128, D], BF16))

                for _tk in tk_qst_fin + tk_kst_fin + tk_vst_fin:
                    S_.wait("g", _tk)
                    S_.wait("y", _tk)
                    S_.wait("v", _tk)

                S_.wait("g", tk_bias)
                for dc in range(8):
                    srcw = wout.rearrange("(dc p) n -> dc p n", p=128)[dc]
                    dstw = wout_sb[:, dc * D:(dc + 1) * D]
                    S_.g(lambda s=srcw, d=dstw: nc.gpsimd.dma_start(d, s))
                    tk_wout = S_.mark("gd_wout")

                ffn = {"wfc": [None, None], "wfc_use": [None, None],
                       "wpj": [None, None], "wpj_use": [None, None],
                       "hev": [None, None], "sq_last": None,
                       "yev_last": None, "hall": None, "ybank": None,
                       "spk_end": None}

                def load_wfc(hf):
                    if hf >= NOWN * 8:
                        return
                    fb2, g = divmod(hf, 8)
                    par = hf % 2
                    S_.wait("y", ffn["wfc_use"][par])
                    S_.wait("y", tk_wcast)
                    if hf <= 1:
                        S_.wait("y", tk_bias)
                    srcf = wfcb.rearrange("(dc p) n -> p dc n", p=128)[
                        :, :, g * 512:(g + 1) * 512]
                    dstf = wfc_sb[:, par * 4096:(par + 1) * 4096].rearrange(
                        "p (dc c) -> p dc c", dc=8)
                    S_.y(lambda s=srcf, d=dstf: nc.sync.dma_start(d, s))
                    ffn["wfc"][par] = S_.mark("y_wfc0" if par == 0 else "y_wfc1")

                def load_wpj(pf):
                    if pf >= NOWN * 4 * 16:
                        return
                    g = pf % 16
                    par = pf % 2
                    S_.wait("g", ffn["wpj_use"][par])
                    S_.wait("g", tk_wcast)
                    S_.wait("g", tk_bias)
                    srcp = wpjb.rearrange("(f p) n -> p f n", p=128)[
                        :, g * 2:(g + 1) * 2, :]
                    dstp = wpj_sb[:, par * 2048:(par + 1) * 2048].rearrange(
                        "p (f c) -> p f c", f=2)
                    S_.g(lambda s=srcp, d=dstp: nc.gpsimd.dma_start(d, s))
                    ffn["wpj"][par] = S_.mark("gd_wpj0" if par == 0 else "gd_wpj1")

                def h_unit(fb, fc, half):
                    hf = fb * 8 + fc // 4
                    par = hf % 2
                    hb = 6 + (fc % 2)
                    if hf == 0 and fc == 0 and half == 0:
                        load_wfc(0)
                        load_wfc(1)
                    S_.wait("t", ffn["wfc"][par])
                    if half == 0:
                        S_.wait("t", ffn["hev"][fc % 2])
                    if fb == 0 and fc == 0:
                        S_.wait("t", tk_bias)
                    for di in range(half * 4, half * 4 + 4):
                        lhs = wfc_sb[:, par * 4096 + di * 512 + (fc % 4) * 128:
                                     par * 4096 + di * 512 + (fc % 4 + 1) * 128]
                        rhs = xn_own[:, fb * 4096 + di * 512:
                                     fb * 4096 + (di + 1) * 512]
                        S_.t(lambda l=lhs, r=rhs, b=hb, st=(di == 0), sp=(di == 7):
                             MM(bank(b), l, r, start=st, stop=sp))
                    tk_hmm = S_.mark("t")
                    if half == 0:
                        return
                    if fc % 4 == 3:
                        ffn["wfc_use"][par] = tk_hmm
                        load_wfc(hf + 2)
                    S_.wait("v", tk_hmm)
                    S_.wait("v", ffn["ybank"])
                    S_.wait("v", ffn["sq_last"])
                    hro = h_sb[:, fc * 512:(fc + 1) * 512]
                    S_.v(lambda o=hro, b=hb: nc.vector.tensor_scalar(
                        o, bank(b), 0.0, None, ALU.max))
                    t_ = S_.mark("v")
                    ffn["hev"][fc % 2] = t_
                    S_.wait("v", t_)
                    S_.v(lambda o=hro: nc.vector.tensor_mul(o, o, o))
                    ffn["sq_last"] = S_.mark("v")
                    if fc == 31:
                        ffn["hall"] = ffn["sq_last"]

                def y_unit(fb, tq, g, half):
                    pf = (fb * 4 + tq) * 16 + g
                    par = pf % 2
                    if pf == 0 and half == 0:
                        load_wpj(0)
                        load_wpj(1)
                    S_.wait("t", ffn["wpj"][par])
                    S_.wait("t", ffn["hall"])
                    if g == 0 and half == 0:
                        S_.wait("t", ffn["ybank"])
                    for f2 in (half,):
                        f = g * 2 + f2
                        lhs = h_sb[:, f * 512 + tq * 128: f * 512 + (tq + 1) * 128]
                        for nn in range(2):
                            rhs = wpj_sb[:, par * 2048 + f2 * 1024 + nn * 512:
                                         par * 2048 + f2 * 1024 + (nn + 1) * 512]
                            b = 6 + nn
                            S_.t(lambda l=lhs, r=rhs, b_=b,
                                 st=(f == 0), sp=(f == 31):
                                 MM(bank(b_), l, r, start=st, stop=sp))
                    tk_ymm = S_.mark("t")
                    if half == 0:
                        return
                    ffn["wpj_use"][par] = tk_ymm
                    load_wpj(pf + 2)
                    if g == 15:
                        gtq = fb * 4 + tq
                        S_.wait("v", tk_ymm)
                        S_.wait("v", ffn["spk_end"])
                        S_.v(lambda: nc.vector.tensor_copy(y_sb[:, 0:512], bank(6)))
                        S_.v(lambda: nc.vector.tensor_copy(y_sb[:, 512:1024], bank(7)))
                        t_ = S_.mark("v")
                        ffn["ybank"] = t_
                        S_.wait("v", t_)
                        S_.v(lambda: nc.vector.tensor_scalar(
                            ssum_sb[:], y_sb[:], 1.0, None, ALU.is_gt))
                        t_ = S_.mark("v")
                        S_.wait("v", t_)
                        S_.v(lambda: nc.vector.scalar_tensor_tensor(
                            mem_sb[:], y_sb[:], 1.9, ssum_sb[:], ALU.mult,
                            ALU.subtract))
                        t_ = S_.mark("v")
                        S_.wait("v", t_)
                        S_.v(lambda: nc.vector.tensor_scalar(
                            spk_sb[:], mem_sb[:], 1.0, None, ALU.is_gt))
                        t_ = S_.mark("v")
                        S_.wait("v", t_)
                        S_.v(lambda: nc.vector.tensor_add(ssum_sb[:], ssum_sb[:],
                                                          spk_sb[:]))
                        t_ = S_.mark("v")
                        for stp in range(3):
                            S_.wait("v", t_)
                            S_.v(lambda: nc.vector.scalar_tensor_tensor(
                                y_sb[:] if False else spk_sb[:], mem_sb[:], 0.9,
                                y_sb[:], ALU.mult, ALU.add) if False else
                                nc.vector.scalar_tensor_tensor(
                                    mem_sb[:], mem_sb[:], 0.9, y_sb[:],
                                    ALU.mult, ALU.add))
                            t_ = S_.mark("v")
                            S_.wait("v", t_)
                            S_.v(lambda: nc.vector.tensor_sub(mem_sb[:], mem_sb[:],
                                                              spk_sb[:]))
                            t_ = S_.mark("v")
                            S_.wait("v", t_)
                            S_.v(lambda: nc.vector.tensor_scalar(
                                spk_sb[:], mem_sb[:], 1.0, None, ALU.is_gt))
                            t_ = S_.mark("v")
                            S_.wait("v", t_)
                            S_.v(lambda: nc.vector.tensor_add(ssum_sb[:], ssum_sb[:],
                                                              spk_sb[:]))
                            t_ = S_.mark("v")
                        S_.wait("v", t_)
                        S_.wait("v", ffn.get("ss_dma"))
                        S_.v(lambda: nc.vector.tensor_copy(ssb_st[:], ssum_sb[:]))
                        t_ = S_.mark("v")
                        ffn["spk_end"] = t_
                        S_.wait("y", t_)
                        S_.y(lambda g_=gtq: nc.sync.dma_start(
                            sscr[g_ * 128:(g_ + 1) * 128, :], ssb_st[:]))
                        ffn["ss_dma"] = S_.mark("y_ss")

                def h_full(fb, fc):
                    h_unit(fb, fc, 0)
                    h_unit(fb, fc, 1)

                def y_full(fb, tq, g):
                    y_unit(fb, tq, g, 0)
                    y_unit(fb, tq, g, 1)

                op_state = {"edone": None, "xo": None, "ssl": None,
                            "out": [None, None], "oev": None}

                def outproj_unit(gtq):
                    S_.wait("t", ffn["hev"][0])
                    S_.wait("t", ffn["hev"][1])
                    S_.wait("t", ffn["ybank"])
                    S_.wait("t", tk_wout)
                    S_.wait("t", op_state["oev"])
                    S_.wait("t", tk_bb)
                    for hp2 in range(8):
                        lhs = o_sb[:, hp2 * T + gtq * 128: hp2 * T + (gtq + 1) * 128]
                        for nn in range(2):
                            rhs = wout_sb[:, hp2 * D + nn * 512:
                                          hp2 * D + (nn + 1) * 512]
                            S_.t(lambda l=lhs, r=rhs, b=6 + nn,
                                 st=(hp2 == 0), sp=(hp2 == 7):
                                 MM(bank(b), l, r, start=st, stop=sp))
                    tk_at = S_.mark("t")
                    S_.wait("y", op_state["edone"])
                    S_.wait("y", ffn["ss_dma"])
                    S_.y(lambda g_=gtq: nc.sync.dma_start(
                        x_tm[:], x_own[g_ * 128:(g_ + 1) * 128, :]))
                    tk_xo = S_.mark("y_xtm")
                    S_.y(lambda g_=gtq: nc.sync.dma_start(
                        ss_ld[:], sscr[g_ * 128:(g_ + 1) * 128, :]))
                    tk_ssl = S_.mark("y_ss0")
                    S_.wait("v", op_state["out"][0])
                    S_.wait("v", op_state["out"][1])
                    S_.wait("v", tk_at)
                    S_.wait("v", tk_bb)
                    S_.v(lambda: nc.vector.tensor_add(eo_sb[:], bb_sb[:], bank(6, 2)))
                    tk_eo = S_.mark("v")
                    ffn["hev"][0] = tk_eo
                    ffn["hev"][1] = tk_eo
                    ffn["ybank"] = tk_eo
                    S_.wait("v", tk_eo)
                    S_.wait("v", tk_xo)
                    S_.v(lambda: nc.vector.tensor_add(eo_sb[:], eo_sb[:], x_tm[:]))
                    t_ = S_.mark("v")
                    S_.wait("v", t_)
                    S_.wait("v", tk_ssl)
                    S_.v(lambda: nc.vector.scalar_tensor_tensor(
                        eo_sb[:], ss_ld[:], 0.2, eo_sb[:], ALU.mult, ALU.add))
                    tk_eo = S_.mark("v")
                    op_state["edone"] = tk_eo
                    S_.wait("y", tk_eo)
                    S_.y(lambda g_=gtq: nc.sync.dma_start(
                        out_d[g_ * 128:(g_ + 1) * 128, :], eo_sb[:]))
                    op_state["out"][gtq % 2] = S_.mark(
                        "y_out0" if gtq % 2 == 0 else "y_out1")

                ffn_units = []
                for fb in range(NOWN):
                    for fc in range(32):
                        ffn_units.append(lambda fb_=fb, fc_=fc: h_full(fb_, fc_))
                    for tq in range(4):
                        for g in range(16):
                            ffn_units.append(lambda fb_=fb, tq_=tq, g_=g:
                                             y_full(fb_, tq_, g_))
                ffn_i = [0]
                n_steps = 8 * (T // 512) * KC
                stride = max(n_steps // (len(ffn_units) + 1), 1)
                step_no = [0]

                def pump():
                    step_no[0] += 1
                    while ffn_i[0] < len(ffn_units) and \
                            ffn_i[0] * stride < step_no[0]:
                        ffn_units[ffn_i[0]]()
                        ffn_i[0] += 1

                NQB = T // 512
                NKCG = KC // 16
                assert NKCG * 16 == KC
                flat = [(hp, qb, kcg) for qb in range(NQB) for hp in range(8)
                        for kcg in range(NKCG)]
                tk_k = {}
                tk_v = {}
                tk_q = [None, None]
                tk_k_use = {}
                tk_v_use = {}
                tk_q_use = [None, None]
                tk_oev = None
                tk_last_sE = None
                tk_last_sE2 = None
                tk_oevac = None
                tk_rbcc = [None, None]

                def load_kv(fi):
                    if fi >= len(flat):
                        return
                    hp2, qb2, kcg2 = flat[fi]
                    par = fi % 3
                    S_.wait("y", tk_k_use.get(fi - 3))
                    if fi <= 2:
                        S_.wait("y", tk_bias)
                        S_.wait("y", tk_kst_fin[0])
                        S_.wait("y", tk_kst_fin[1])
                    ksrc = kscr[hp2 * 128:(hp2 + 1) * 128,
                                kcg2 * 2048:(kcg2 + 1) * 2048]
                    S_.y(lambda d=kch_sb[:, par * 2048:(par + 1) * 2048], s=ksrc:
                         nc.sync.dma_start(d, s))
                    tk_k[fi] = S_.mark(("y_k0", "y_k1", "y_k2")[par])
                    S_.wait("g", tk_v_use.get(fi - 3))
                    if fi <= 2:
                        S_.wait("g", tk_bias)
                        S_.wait("g", tk_vst_fin[0])
                        S_.wait("g", tk_vst_fin[1])
                        vb0 = v65_sb[:, par * 2176:(par + 1) * 2176]
                        S_.g(lambda d=vb0: nc.gpsimd.memset(d, 1.0))
                        tk_vms = S_.mark("gc")
                        S_.wait("g", tk_vms)
                    vb = v65_sb[:, par * 2176:(par + 1) * 2176]
                    for hh in range(2):
                        vsrc = vscr.rearrange("(kc p) (hh d) -> p kc hh d",
                                              p=128, d=64)[
                            :, kcg2 * 16:(kcg2 + 1) * 16, 2 * hp2 + hh, :]
                        vdst = vb.rearrange("p (kc hh c) -> p kc hh c",
                                            kc=16, hh=2)[:, :, hh, 0:64]
                        S_.g(lambda s=vsrc, d=vdst: nc.gpsimd.dma_start(d, s))
                        tk_v[fi] = S_.mark(("gd_v0", "gd_v1", "gd_v2")[par])

                def load_q(hp2, qb2):
                    par = (qb2 * 8 + hp2) % 2
                    S_.wait("y", tk_q_use[par])
                    if hp2 == 0 and qb2 == 0:
                        S_.wait("y", tk_qst_fin[0])
                        S_.wait("y", tk_qst_fin[1])
                    qsrc = qscr[hp2 * 128:(hp2 + 1) * 128,
                                qb2 * 512:(qb2 + 1) * 512]
                    S_.y(lambda d=qv_sb[:, par * 512:(par + 1) * 512], s=qsrc:
                         nc.sync.dma_start(d, s))
                    tk_q[par] = S_.mark("y_q0" if par == 0 else "y_q1")

                load_q(0, 0)
                load_kv(0)
                load_kv(1)
                load_kv(2)
                for qb in range(NQB):
                    for hp in range(8):
                        it = qb * 8 + hp
                        pq = it % 2
                        fbase = it * NKCG
                        if it + 1 < 8 * NQB:
                            load_q(*flat[(it + 1) * NKCG][0:2])
                        qv = qv_sb[:, pq * 512:(pq + 1) * 512]
                        tS = {}
                        sE = {}
                        tPV = {}

                        def emit_S(kc):
                            par = kc % 2
                            kpar = (fbase + kc // 16) % 3
                            ksl = kch_sb[:, kpar * 2048 + (kc % 16) * 128:
                                         kpar * 2048 + (kc % 16 + 1) * 128]
                            S_.t(lambda l=ksl[0:64, :], r=qv[0:64, :], p_=par:
                                 MM(bank(2 * p_), l, r, start=True, stop=True))
                            S_.t(lambda l=ksl[64:128, :], r=qv[64:128, :], p_=par:
                                 MM(bank(2 * p_ + 1), l, r, start=True, stop=True))
                            tS[kc] = S_.mark("t")

                        S_.wait("t", tk_q[pq])
                        S_.wait("t", tk_k[fbase])
                        S_.wait("t", tk_last_sE2)
                        emit_S(0)
                        S_.wait("t", tk_rbcc[0])
                        S_.wait("t", tk_rbcc[1])
                        S_.wait("t", tk_last_sE)
                        if KC > 16:
                            S_.wait("t", tk_k[fbase + 1])
                        emit_S(1)
                        for kc in range(KC):
                            par = kc % 2
                            fi = fbase + kc // 16
                            S_.wait("s", tS[kc])
                            S_.wait("s", tPV.get(kc - 2))
                            S_.s(lambda o=p_sb[:, par * 1024:(par + 1) * 1024],
                                 p_=par: nc.scalar.activation(
                                     o, ps[:, p_ * 1024:(p_ + 1) * 1024], AF.Exp,
                                     bias=bias_sb[:]))
                            sE[kc] = S_.mark("s")
                            kpar = fi % 3
                            vb = v65_sb[:, kpar * 2176:kpar * 2176 + 2176]
                            vs0 = vb[:, (kc % 16) * 136:(kc % 16) * 136 + 65]
                            vs1 = vb[:, (kc % 16) * 136 + 68:(kc % 16) * 136 + 133]
                            sta = (kc == 0)
                            spa = (kc == KC - 1)
                            S_.wait("t", sE[kc])
                            if kc == 0:
                                S_.wait("t", tk_oevac)
                            if kc % 16 == 0:
                                S_.wait("t", tk_v[fi])
                            S_.t(lambda l=vs0, r=p_sb[:, par * 1024:par * 1024 + 512],
                                 a=sta, b=spa:
                                 MM(ps[0:65, 2048:2560], l, r, start=a, stop=b))
                            S_.t(lambda l=vs1,
                                 r=p_sb[:, par * 1024 + 512:par * 1024 + 1024],
                                 a=sta, b=spa:
                                 MM(ps[0:65, 2560:3072], l, r, start=a, stop=b))
                            if kc + 2 < KC:
                                if (kc + 2) % 16 == 0:
                                    S_.wait("t", tk_k[fbase + (kc + 2) // 16])
                                emit_S(kc + 2)
                            tPV[kc] = S_.mark("t")
                            if kc % 16 == 15:
                                tk_k_use[fi] = tS[kc]
                                tk_v_use[fi] = tPV[kc]
                                load_kv(fi + 3)
                            pump()
                        tk_q_use[pq] = tS[KC - 1]
                        tk_last_sE = sE[KC - 1]
                        tk_last_sE2 = sE[KC - 2]
                        odst = o_sb[:, hp * T + qb * 512: hp * T + (qb + 1) * 512]
                        S_.wait("v", tPV[KC - 1])
                        S_.v(lambda d=odst[0:64, :]: nc.vector.tensor_copy(
                            d, ps[0:64, 4 * 512:5 * 512]))
                        S_.v(lambda d=odst[64:128, :]: nc.vector.tensor_copy(
                            d, ps[0:64, 5 * 512:6 * 512]))
                        tk_oevac = S_.mark("v")
                        S_.wait("s", tPV[KC - 1])
                        S_.s(lambda: nc.scalar.activation(
                            rcl2_sb[:], ps[64:65, 4 * 512:6 * 512], AF.Ln))
                        t_ = S_.mark("s")
                        S_.wait("s", t_)
                        S_.s(lambda: nc.scalar.activation(
                            rcl_sb[:], rcl2_sb[:], AF.Exp, scale=-1.0))
                        tk_lrec = S_.mark("s")
                        for j, head in ((0, 0), (1, 1)):
                            S_.wait("t", tk_lrec)
                            S_.wait("t", tk_rbcc[j])
                            if j == 0:
                                S_.wait("t", sE[KC - 1])
                            S_.t(lambda bb_=j: MM(
                                ps[0:128, (2 + bb_) * 512:(2 + bb_) * 512 + 512],
                                ones_f[0:1, 0:128],
                                rcl_sb[0:1, bb_ * 512:(bb_ + 1) * 512],
                                start=True, stop=True))
                            t_ = S_.mark("t")
                            tk_bcast_prev = t_
                            S_.wait("v", t_)
                            S_.wait("v", tk_oev)
                            S_.v(lambda bb_=j: nc.vector.tensor_copy(
                                rbc_sb[:], ps[0:128, (2 + bb_) * 512:(2 + bb_) * 512 + 512]))
                            t_ = S_.mark("v")
                            tk_rbcc[j] = t_
                            S_.wait("v", t_)
                            dsl = odst[0:64, :] if head == 0 else odst[64:128, :]
                            rsl = rbc_sb[0:64, :] if head == 0 else rbc_sb[64:128, :]
                            S_.v(lambda d=dsl, r_=rsl: nc.vector.tensor_mul(
                                d, d, r_))
                            tk_oev = S_.mark("v")
                        if hp == 7:
                            op_state["oev"] = tk_oev
                            for tq2_ in range(4):
                                ffn_units.append(
                                    lambda g_=qb * 4 + tq2_: outproj_unit(g_))
                while ffn_i[0] < len(ffn_units):
                    ffn_units[ffn_i[0]]()
                    ffn_i[0] += 1
                tk_ffn_end_v = ffn["spk_end"]
                tk_ffn_end_t = ("t", S_.cnt["t"])
                tk_p3_end_s = tk_last_sE
                tk_p3_end_v = tk_oev

        for _sem in ("y_const", "gd_mask", "gd_wqkv", "gd_wout", "gd_wfc0",
                     "gd_wfc1", "gd_x0", "gd_x1", "gd_x2", "y_kst0", "y_kst1", "y_vst0",
                     "y_vst1", "y_kld0", "y_kld1", "gd_vld0", "gd_vld1",
                     "gd_wpj0", "gd_wpj1", "y_xtm", "y_out", "gd_wcast",
                     "y_wfc0", "y_wfc1", "y_qst0", "y_qst1", "y_k0", "y_k1",
                     "gd_v0", "gd_v1", "y_q0", "y_q1", "y_ss", "y_ss0",
                     "y_ss1", "y_out0", "y_out1", "y_k2", "gd_v2"):
            if _sem in S_.cnt:
                S_.wait("y", (_sem, S_.cnt[_sem]))

        def make_body(stream):
            def body(eng):
                last = None
                for kind, sem, val in S_.ops[stream]:
                    if kind == "op":
                        r = val()
                        if r is not None:
                            last = r
                    elif kind == "inc":
                        last.then_inc(sems[sem], val)
                    else:
                        eng.wait_ge(sems[sem], val)
            return body

        with nc.Block() as block:
            block.tensor(make_body("t"))
            block.vector(make_body("v"))
            block.scalar(make_body("s"))
            block.gpsimd(make_body("g"))
            block.sync(make_body("y"))

    return nc


def _consts():
    f = np.arange(NF)
    n = np.arange(DH)
    w = np.where((f == 0) | (f == NF - 1), 1.0, 2.0) / (DH * 8.0)
    cosb = np.cos(2 * np.pi * np.outer(f, n) / DH)
    sinb = np.sin(2 * np.pi * np.outer(f, n) / DH)
    cc = np.zeros((128, 128), np.float32)
    cs = np.zeros((128, 128), np.float32)
    cc[0:NF, 0:DH] = cosb
    cc[64:64 + NF, 64:64 + DH] = cosb
    cs[0:NF, 0:DH] = sinb
    cs[64:64 + NF, 64:64 + DH] = sinb
    wv = np.zeros((128, 1), np.float32)
    wv[0:NF, 0] = w
    wv[64:64 + NF, 0] = w
    return cc, cs, wv


def make_in_maps(x, wqkv, wout, bout, mask, wfc, wproj, n_cores=8):
    x = np.asarray(x, np.float32)
    B, S, _ = x.shape
    T = (B * S) // n_cores
    cc, cs, wv = _consts()
    wqkv = np.ascontiguousarray(np.asarray(wqkv, np.float32))
    wout = np.ascontiguousarray(np.asarray(wout, np.float32))
    bout = np.asarray(bout, np.float32).reshape(1, D)
    mask = np.asarray(mask, np.float32).reshape(NF, 1)
    wfcT = np.ascontiguousarray(np.asarray(wfc, np.float32).T)
    wprojT = np.ascontiguousarray(np.asarray(wproj, np.float32).T)
    per_seq = n_cores // B
    in_maps = []
    for c in range(n_cores):
        b, r = divmod(c, per_seq)
        xr = np.roll(x[b], -T * r, axis=0)
        in_maps.append({
            "xT": np.ascontiguousarray(xr.T),
            "x_own": np.ascontiguousarray(xr[0:T]),
            "wqkv": wqkv, "wout": wout, "bout": bout,
            "wfcT": wfcT, "wprojT": wprojT,
            "mask_l": mask, "ccos": cc, "csin": cs, "wvec": wv,
        })
    return in_maps, B, S, T


def kernel(L0_attn_x, L0_attn_w_qkv, L0_attn_w_out, L0_attn_b_out,
           L0_attn_mask_logits, L0_ffn_W_fc, L0_ffn_W_proj):
    in_maps, B, S, T = make_in_maps(
        L0_attn_x, L0_attn_w_qkv, L0_attn_w_out, L0_attn_b_out,
        L0_attn_mask_logits, L0_ffn_W_fc, L0_ffn_W_proj)
    nc = build({"S": S, "T": T})
    res = run_bass_kernel_spmd(nc, in_maps, core_ids=list(range(8)))
    per_seq = 8 // B
    out = np.empty((B, S, D), np.float32)
    for c in range(8):
        b, r = divmod(c, per_seq)
        out[b, r * T:(r + 1) * T] = np.asarray(res.results[c]["out"], np.float32)
    return out
